# version 19
# baseline (speedup 1.0000x reference)
import numpy as np
from contextlib import ExitStack

import concourse.bass as bass
import concourse.mybir as mybir
from concourse.bass_utils import run_bass_kernel_spmd

F32, BF16 = mybir.dt.float32, mybir.dt.bfloat16
AF, OP, AX = mybir.ActivationFunctionType, mybir.AluOpType, mybir.AxisListType

MAGIC = float(np.float32(1.5 * 2 ** 23))
NCORE = 8
B, T, E, I = 16, 2048, 1024, 3072
BPC = B // NCORE
N = BPC * T
EC, IC = E // 128, I // 128
NT = N // 128
NTC = N // 512
QM = 32767.0
NEGI, POSI = -3.0e38, 3.0e38

C_M, C_NM, C_EPS, C_NHALF, C_NONE, C_65536, C_I1024, C_HALF = range(8)
S_S1, S_IS1, S_SG, S_C1, S_NMSG, S_C3, S_C2, S_C4, S_C5, S_SU, S_SG2 = range(11)

ALL_PHASES = ("W", "LN1", "ZQ1", "A1", "B1", "C1", "D1", "ATTN",
              "ZQ2", "A2", "B2", "C2", "D2")


def build(dbg=(), phase_limit=None):
    nc = bass.Bass("TRN2", target_bir_lowering=False)
    phases = ALL_PHASES if phase_limit is None else \
        ALL_PHASES[: ALL_PHASES.index(phase_limit) + 1]
    P = set(phases)

    def dram(name, shape, dtype, kind="Internal"):
        if name in dbg and kind == "Internal":
            kind = "ExternalOutput"
        return nc.dram_tensor(name, shape, dtype, kind=kind).ap()

    x0 = dram("x0", [N, E], F32, "ExternalInput")
    wT = {"up": dram("wupT", [E, I], F32, "ExternalInput"),
          "gate": dram("wgateT", [E, I], F32, "ExternalInput"),
          "down": dram("wdownT", [I, E], F32, "ExternalInput")}
    wNat = {"up": dram("wupN", [I, E], F32, "ExternalInput"),
            "gate": dram("wgateN", [I, E], F32, "ExternalInput"),
            "down": dram("wdownN", [E, I], F32, "ExternalInput")}
    wjT = dram("wjT", [E, E], F32, "ExternalInput")
    wjvT = dram("wjvT", [E, E], F32, "ExternalInput")
    woT = dram("woT", [E, E], F32, "ExternalInput")
    identD = dram("identD", [128, 128], F32, "ExternalInput")
    onesD = dram("onesD", [128, 128], BF16, "ExternalInput")
    masksD = dram("masksD", [128, 4, 512], BF16, "ExternalInput")
    g96D = dram("g96D", [96, I], BF16, "ExternalInput")
    constD = dram("constD", [128, 12], F32, "ExternalInput")

    out = dram("out", [N, E], F32, "ExternalOutput")

    x1s = dram("x1s", [N, E], F32)
    x2s = dram("x2s", [N, E], F32)
    zs = dram("zs", [N, E], BF16)
    gscr = dram("gscr", [IC, 128, N], BF16)
    uscr = dram("uscr", [IC, 128, N], BF16)
    pscr = dram("pscr", [IC, 128, N], BF16)
    oscr = dram("oscr", [NTC, IC, 128, 512], BF16)
    fscr = dram("fscr", [NTC, EC, 128, 512], BF16)
    dq = {"up": dram("dqup", [EC, 128, I], BF16),
          "gate": dram("dqgate", [EC, 128, I], BF16),
          "down": dram("dqdown", [IC, 128, E], BF16)}
    sclr = dram("sclr", [1, 16], F32)
    NBAR = 8
    ccin = [dram(f"ccin{k}", [1, 4], F32) for k in range(NBAR)]
    ccout = [dram(f"ccout{k}", [1, 32], F32) for k in range(NBAR)]

    es = ExitStack()

    def sbt(name, shape, dt, st=None):
        return (st or es).enter_context(nc.sbuf_tensor(name, shape, dt))

    cc = sbt("cc", [128, 12], F32)
    ident = sbt("ident", [128, 128], F32)
    ones_bf = sbt("ones_bf", [128, 128], BF16)
    masks = sbt("masks", [128, 4, 512], BF16)
    g96 = sbt("g96", [96, I], BF16)
    sv = sbt("sv", [1, 16], F32)
    sbc = sbt("sbc", [128, 16], F32)
    tm32 = sbt("tm32", [1, 32], F32)
    tm4 = sbt("tm4", [4, 1], F32)
    sm16 = sbt("sm16", [128, 16], F32)
    zacc = sm16[:, 0:1]
    maccT = sm16[:, 1:2]
    oaccT = sm16[:, 2:3]
    rsm = sm16[:, 3:4]
    aacc = sm16[:, 4:8]
    shp4 = sm16[:, 8:12]
    lnm = [sm16[:, 12:13], sm16[:, 13:14]]
    lnv = [sm16[:, 14:15], sm16[:, 15:16]]
    stg = [sbt(f"stg{i}", [128, 2048], BF16) for i in range(2)]

    LNS = {}

    ln_n = [0]

    def ln_alloc(st):
        u = ln_n[0]
        ln_n[0] += 1
        LNS["xt"] = [sbt(f"xt{u}_{i}", [128, E], F32, st) for i in range(2)]
        LNS["xr"] = [sbt(f"xr{u}_{i}", [128, E], F32, st) for i in range(2)]
        LNS["xq"] = [sbt(f"xq{u}_{i}", [128, E], F32, st) for i in range(2)]
        LNS["ztl"] = [sbt(f"ztl{u}_{i}", [128, E], BF16, st) for i in range(2)]
        LNS["ftm"] = sbt(f"ftm{u}", [128, 4, E], BF16, st)

    psum = es.enter_context(nc.psum_tensor("pb", [128, 4096], F32))

    def bank(i):
        return psum[:, 512 * i: 512 * (i + 1)]

    sems, cnt = {}, {}
    for s in ("qS", "qA", "tT", "tV", "tS", "tG", "tC"):
        sems[s] = es.enter_context(nc.semaphore(s))
        cnt[s] = 0

    def inc(inst, s, n=1):
        if s in ("tV", "tS", "tG"):
            return cnt[s]
        inst.then_inc(sems[s], n)
        cnt[s] += n
        return cnt[s]

    def dma(eng, out_, in_, q="qS"):
        return inc(eng.dma_start(out=out_, in_=in_), q, 16)

    def dmaT(eng, out_, in_, q="qS"):
        return inc(eng.dma_start_transpose(out=out_, in_=in_), q, 16)

    def wge(eng, s, v=None):
        eng.wait_ge(sems[s], cnt[s] if v is None else v)

    def wall(eng):
        for s in sems:
            if cnt[s]:
                wge(eng, s)

    blk = es.enter_context(nc.Block())

    pend = {e: [] for e in ("sync", "gpsimd", "tensor", "vector", "scalar")}
    _OWN = {"vector": "tV", "scalar": "tS", "gpsimd": "tG"}
    _NOSELF = ("dma_start", "dma_start_transpose", "collective_compute")

    class FakeInst:
        def __init__(self, item):
            self.item = item

        def then_inc(self, sem, n):
            self.item[2].append((sem, n))
            return self

    class RecEng:
        def __init__(self, name):
            self._n = name
            self._own = _OWN.get(name)

        def wait_ge(self, sem, val):
            pend[self._n].append(("wait", sem, val))

        def _raw(self, closure, selfsync=True):
            incs = []
            if self._own is not None:
                pend[self._n].append(("wait", sems[self._own], cnt[self._own]))
                if selfsync:
                    incs.append((sems[self._own], 1))
                    cnt[self._own] += 1
            item = ["op", closure, incs]
            pend[self._n].append(item)
            return FakeInst(item)

        def __getattr__(self, a):
            if a.startswith("_"):
                raise AttributeError(a)

            def call(*args, **kw):
                return self._raw(
                    lambda E, a=a, args=args, kw=kw: getattr(E, a)(*args, **kw),
                    selfsync=a not in _NOSELF)
            return call

    RECS = {e: RecEng(e) for e in pend}
    GROUPED = ()

    def on(engine, fn):
        fn(RECS[engine])
        flush(only=[e for e in pend if e not in GROUPED])

    def flush(only=None):
        for e in ("sync", "gpsimd", "tensor", "vector", "scalar"):
            if only is not None and e not in only:
                continue
            q = pend[e]
            if not q:
                continue
            pend[e] = []

            def body(E, q=q):
                for it in q:
                    if it[0] == "wait":
                        E.wait_ge(it[1], it[2])
                    else:
                        inst = it[1](E)
                        for sem, n in it[2]:
                            inst.then_inc(sem, n)
            getattr(blk, e)(body)

    def SV(c):
        return sv[0:1, c:c + 1]

    def SB(c):
        return sbc[:, c:c + 1]

    def CCc(c):
        return cc[:, c:c + 1]

    def TM(c):
        return tm32[0:1, c:c + 1]

    MM = dict(skip_group_check=True)

    def acc_init(g):
        g.memset(zacc, 0.0)
        g.memset(maccT, 0.0)
        g.memset(oaccT, 0.0)
        g.memset(sm16[:, 4:5], 0.0)
        return g.memset(sm16[:, 5:6], 0.0)

    def ph_setup():
        def sy(sy):
            dma(sy, cc[:], constD)
            dma(sy, ident[:], identD)
            dma(sy, ones_bf[:], onesD)
            dma(sy, masks[:], masksD)
            dma(sy, g96[:], g96D)
        on("sync", sy)

        def g(g):
            inc(acc_init(g), "tG")
        on("gpsimd", g)

    bar_i = [0]

    def barrier(ship_ap, nval, derive_fn):
        flush()
        k = bar_i[0]
        bar_i[0] += 1

        def t(t):
            wall(t)
            inc(t.transpose(psum[0:nval, 0:128], ship_ap, ident[:]), "tT")
        on("tensor", t)

        def v(v):
            wge(v, "tT")
            inc(v.tensor_reduce(tm4[0:nval, 0:1], psum[0:nval, 0:128],
                                AX.X, OP.max), "tV")
        on("vector", v)

        def sy(sy):
            wge(sy, "tV")
            dma(sy, ccin[k][:, 0:nval], tm4[0:nval, 0:1])
        on("sync", sy)

        def g(g):
            wge(g, "qS")
            inc(g.collective_compute(
                "AllGather", OP.bypass,
                replica_groups=[list(range(NCORE))],
                ins=[ccin[k][:, 0:nval]], outs=[ccout[k][:, 0:8 * nval]]), "tC")
        on("gpsimd", g)

        def sy2(sy):
            wge(sy, "tC")
            dma(sy, tm32[0:1, 0:8 * nval], ccout[k][:, 0:8 * nval])
        on("sync", sy2)

        def v2(v):
            wge(v, "qS")
            for r in range(1, 8):
                v.tensor_tensor(tm32[0:1, 0:nval], tm32[0:1, 0:nval],
                                tm32[0:1, nval * r: nval * (r + 1)], OP.max)
            inc(derive_fn(v), "tV")
        on("vector", v2)

        def sy3(sy):
            wge(sy, "tV")
            dma(sy, sclr, sv[0:1, 0:16])
            wge(sy, "qS")
            dma(sy, sbc[:], sclr[0:1, :].partition_broadcast(128))
        on("sync", sy3)
        flush()

    def derive_z(v):
        v.tensor_scalar(SV(S_S1), TM(0), 1.0 / QM, 1e-12, OP.mult, OP.max)
        return v.reciprocal(SV(S_IS1), SV(S_S1))

    def derive_a(v):
        v.tensor_tensor(TM(8), TM(0), SV(S_S1), OP.mult)
        v.tensor_scalar(TM(8), TM(8), 1.0 / QM, 1e-12, OP.mult, OP.max)
        v.tensor_tensor(TM(9), TM(1), SV(S_S1), OP.mult)
        v.tensor_scalar(TM(9), TM(9), 1.0 / QM, 1e-12, OP.mult, OP.max)
        v.tensor_copy(SV(S_SG), TM(9))
        v.tensor_copy(SV(S_SU), TM(8))
        v.reciprocal(TM(10), TM(8))
        v.reciprocal(TM(11), TM(9))
        v.tensor_tensor(SV(S_C3), SV(S_S1), TM(10), OP.mult)
        v.tensor_tensor(SV(S_C1), SV(S_S1), TM(11), OP.mult)
        return v.tensor_scalar(SV(S_NMSG), SV(S_SG), -MAGIC, None, OP.mult)

    def derive_p(v):
        v.tensor_tensor(TM(1), TM(0), SV(S_SG), OP.mult)
        v.tensor_scalar(TM(1), TM(1), 1.0 / 65536.0, None, OP.mult)
        v.tensor_scalar(TM(2), TM(1), 1.0 / QM, 1e-12, OP.mult, OP.max)
        v.tensor_copy(SV(S_SG2), TM(2))
        v.reciprocal(TM(3), TM(2))
        v.tensor_scalar(TM(4), SV(S_SG), 1.0 / 65536.0, None, OP.mult)
        return v.tensor_tensor(SV(S_C2), TM(4), TM(3), OP.mult)

    def derive_o(v):
        v.tensor_tensor(TM(1), SV(S_SG2), SV(S_SU), OP.mult)
        v.tensor_tensor(TM(2), TM(0), TM(1), OP.mult)
        v.tensor_scalar(TM(3), TM(2), 1.0 / QM, 1e-12, OP.mult, OP.max)
        v.reciprocal(TM(4), TM(3))
        v.tensor_tensor(SV(S_C4), TM(1), TM(4), OP.mult)
        return v.tensor_scalar(SV(S_C5), TM(3), 0.5, None, OP.mult)

    def ph_w(wup_sb, wgate_sb, side_steps=None):
        def side():
            if side_steps:
                side_steps.pop(0)()
        ws = ExitStack()
        smax = sbt("smax", [128, 768], F32, ws)
        srtm = sbt("srtm", [128, 768], F32, ws)
        sTb = sbt("sTb", [96, I], BF16, ws)
        rsTb = sbt("rsTb", [96, I], BF16, ws)
        wtr = sbt("wtr", [128, 2, I], F32, ws)
        dqt = sbt("dqt", [128, 2, I], BF16, ws)
        rtm = sbt("rtm", [128, 512], F32, ws)

        for wname, sdst in (("up", wup_sb), ("gate", wgate_sb), ("down", None)):
            O, In = (I, E) if wname != "down" else (E, I)
            G = In // 32
            noc, nic, nsl = O // 128, In // 128, O // 512

            for oc in range(noc):
                def sy(sy, oc=oc):
                    wge(sy, "tV")
                    if oc == 0:
                        wall(sy)
                    dma(sy, wtr[:, 0, 0:In], wNat[wname][128 * oc:128 * (oc + 1), :])
                on("sync", sy)

                def v(v, oc=oc, G=G):
                    wge(v, "qS")
                    src = wtr[:, 0, 0:In].rearrange("p (g k) -> p g k", k=32)
                    inc(v.tensor_reduce(smax[:, G * oc:G * (oc + 1)], src,
                                        AX.X, OP.max,
                                        apply_absolute_value=True), "tV")
                on("vector", v)
                side()

            def v(v, G=G, noc=noc):
                inc(v.tensor_scalar(srtm[:, 0:G * noc], smax[:, 0:G * noc],
                                    1.0 / 7.0, 1e-12, OP.mult, OP.max), "tV")
            on("vector", v)

            for oc in range(noc):
                def t(t, oc=oc, G=G):
                    wge(t, "tV")
                    inc(t.transpose(psum[0:G, 2048:2176],
                                    srtm[:, G * oc:G * (oc + 1)],
                                    ident[:]), "tT")
                on("tensor", t)

                def v(v, oc=oc, G=G):
                    wge(v, "tT")
                    v.tensor_copy(sTb[0:G, 128 * oc:128 * (oc + 1)],
                                  psum[0:G, 2048:2176])

                    def _rcp(E, G=G, oc=oc):
                        with nc.allow_low_precision(
                                reason="bf16 block-scale expansion operand"):
                            return E.reciprocal(
                                rsTb[0:G, 128 * oc:128 * (oc + 1)],
                                psum[0:G, 2048:2176])
                    v._raw(_rcp)
                on("vector", v)

            rt = {"v": [0, 0], "qa": [0, 0], "q": [0, 0]}
            stt_tick = {}
            gslc = [0]
            for ic in range(nic):
                par = ic % 2

                def sy(sy, ic=ic, par=par, O=O):
                    if ic >= 2:
                        wge(sy, "tV", rt["v"][par])
                    elif ic == 0:
                        wge(sy, "tV")
                    dma(sy, wtr[:, par, 0:O],
                        wT[wname][128 * ic:128 * (ic + 1), :])
                    rt["q"][par] = cnt["qS"]
                on("sync", sy)

                for sl in range(nsl):
                    g_ = gslc[0]
                    gslc[0] += 1
                    pb = 2 * (g_ % 3)

                    def t(t, ic=ic, par=par, sl=sl, G=G, g_=g_, pb=pb):
                        if sl == 0:
                            wge(t, "qS", rt["q"][par])
                            if ic >= 2:
                                wge(t, "qA", rt["qa"][par])
                            elif ic == 0:
                                wge(t, "tV")
                        if stt_tick.get(g_ - 3) is not None:
                            wge(t, "tV", stt_tick[g_ - 3])
                        gsl = g96[0:G, 128 * ic:128 * (ic + 1)]
                        t.matmul(bank(pb), gsl,
                                 sTb[0:G, 512 * sl:512 * (sl + 1)],
                                 start=True, stop=True, **MM)
                        inc(t.matmul(bank(pb + 1), gsl,
                                     rsTb[0:G, 512 * sl:512 * (sl + 1)],
                                     start=True, stop=True, **MM), "tT")
                    on("tensor", t)

                    def v(v, par=par, sl=sl, g_=g_, pb=pb):
                        wge(v, "tT")
                        s_ = slice(512 * sl, 512 * (sl + 1))
                        v.tensor_tensor(rtm[:, 0:512], wtr[:, par, s_],
                                        bank(pb + 1), OP.mult)
                        v.tensor_scalar(rtm[:, 0:512], rtm[:, 0:512], MAGIC,
                                        None, OP.add)
                        inc(v.scalar_tensor_tensor(
                            dqt[:, par, s_], rtm[:, 0:512], MAGIC,
                            bank(pb), OP.subtract, OP.mult), "tV")
                        stt_tick[g_] = cnt["tV"]
                        rt["v"][par] = cnt["tV"]
                    on("vector", v)

                def s2(s, ic=ic, par=par, wname=wname, sdst=sdst, O=O):
                    wge(s, "tV", rt["v"][par])
                    dma(s, dq[wname][ic], dqt[:, par, 0:O], "qA")
                    rt["qa"][par] = cnt["qA"]
                    if sdst is not None:
                        inc(s.copy(sdst[:, ic, 0:O], dqt[:, par, 0:O]), "tS")
                on("scalar", s2)
                side()

            def vfin(v):
                wge(v, "tS")
                wge(v, "qA", max(rt["qa"]))
                inc(v.engine_nop(), "tV")
            on("vector", vfin)

        ws.close()

    lntk = {"z": [0, 0]}

    def ln_tile(par, src_tile, zdst_row, track):
        def v(v):
            inc(v.tensor_reduce(lnm[par], src_tile[:], AX.X, OP.add), "tV")
        on("vector", v)

        def v2(v):
            v.tensor_tensor(LNS["xq"][par][:], src_tile[:], src_tile[:],
                            OP.mult)
            v.tensor_reduce(lnv[par], LNS["xq"][par][:], AX.X, OP.add)
            v.tensor_scalar(lnm[par], lnm[par], 1.0 / E, None, OP.mult)
            v.tensor_tensor(rsm, lnm[par], lnm[par], OP.mult)
            inc(v.scalar_tensor_tensor(lnv[par], lnv[par], 1.0 / E,
                                       rsm, OP.mult, OP.subtract), "tV")
        on("vector", v2)

        def s2(s):
            wge(s, "tV")
            s.activation(lnv[par], lnv[par], AF.Ln, bias=CCc(C_EPS))
            inc(s.activation(lnv[par], lnv[par], AF.Exp,
                             scale=CCc(C_NHALF)), "tS")
        on("scalar", s2)

        def v3(v):
            wge(v, "tS")
            if lntk["z"][par]:
                wge(v, "qA", lntk["z"][par])
            i = v.tensor_scalar(LNS["ztl"][par][:], src_tile[:], lnm[par],
                                lnv[par], OP.subtract, OP.mult)
            if track:
                v.tensor_reduce(rsm, LNS["ztl"][par][:], AX.X, OP.max,
                                apply_absolute_value=True)
                i = v.tensor_tensor(zacc, zacc, rsm, OP.max)
            inc(i, "tV")
        on("vector", v3)

        def s3(s):
            wge(s, "tV")
            dma(s, zs[zdst_row:zdst_row + 128, :], LNS["ztl"][par][:], "qA")
            lntk["z"][par] = cnt["qA"]
        on("scalar", s3)

    def ph_ln1_steps():
        rt = {"v": [0, 0]}
        steps = []

        def step(j):
            par = j % 2

            def sy(sy, j=j, par=par):
                if j >= 2:
                    wge(sy, "tV", rt["v"][par])
                dma(sy, LNS["xt"][par][:], x0[128 * j:128 * (j + 1), :])
            on("sync", sy)

            def v0(v):
                wge(v, "qS")
                inc(v.engine_nop(), "tV")
            on("vector", v0)

            ln_tile(par, LNS["xt"][par], 128 * j, True)
            rt["v"][par] = cnt["tV"]
        return [lambda j=j: step(j) for j in range(NT)]

    def ph_zq(zq_fm, zft, zqt):
        rt = {"v": [0, 0]}
        for c in range(EC):
            for h in range(2):
                idx = 2 * c + h
                par = idx % 2

                def sy(sy, c=c, h=h, par=par, idx=idx):
                    if idx == 0:
                        wall(sy)
                    if idx >= 2:
                        wge(sy, "tV", rt["v"][par])
                    zin = zs[2048 * h:2048 * (h + 1), 128 * c:128 * (c + 1)]
                    dmaT(sy, zft[par][:], zin)
                on("sync", sy)

                def v(v, c=c, h=h, par=par):
                    wge(v, "qS")
                    v.tensor_scalar(zqt[:], zft[par][:], SB(S_IS1),
                                    MAGIC, OP.mult, OP.add)
                    inc(v.tensor_scalar(zq_fm[:, c, 2048 * h:2048 * (h + 1)],
                                        zqt[:], MAGIC, None, OP.subtract),
                        "tV")
                    rt["v"][par] = cnt["tV"]
                on("vector", v)

    def ph_a(zq_fm, wsb_up, wsb_gate):
        groups = []
        for wname, wsb, scr_, cmax, cmin in (
                ("up", wsb_up, uscr, 4, 0), ("gate", wsb_gate, gscr, 5, 0)):
            for m in range(IC):
                for h in range(2):
                    groups.append((wsb, scr_, cmax, cmin, m, h))
        tick_t, tick_s, tick_v, tick_qa = {}, {}, {}, {}
        for gi, (wsb, scr_, cmax, cmin, m, h) in enumerate(groups):
            def t(t, gi=gi, wsb=wsb, m=m, h=h):
                if gi == 0:
                    wall(t)
                if gi >= 2:
                    wge(t, "tV", tick_v[gi - 2])
                for k in range(EC):
                    lhs = wsb[:, k, 128 * m:128 * (m + 1)]
                    for tc4 in range(4):
                        tcg = 4 * h + tc4
                        i = t.matmul(bank(4 * (gi % 2) + tc4), lhs,
                                     zq_fm[:, k, 512 * tcg:512 * (tcg + 1)],
                                     start=(k == 0), stop=(k == EC - 1), **MM)
                inc(i, "tT")
                tick_t[gi] = cnt["tT"]
            on("tensor", t)

            def v(v, gi=gi, cmax=cmax, cmin=cmin):
                wge(v, "tT", tick_t[gi])
                if gi >= 2:
                    wge(v, "qA", tick_qa[gi - 2])
                half = psum[:, 2048 * (gi % 2):2048 * (gi % 2) + 2048]
                v.tensor_copy(stg[gi % 2][:], half)
                v.tensor_reduce(rsm, half, AX.X, OP.max,
                                apply_absolute_value=True)
                inc(v.tensor_tensor(sm16[:, cmax:cmax + 1],
                                    sm16[:, cmax:cmax + 1], rsm, OP.max),
                    "tV")
                tick_v[gi] = cnt["tV"]
                tick_s[gi] = cnt["tV"]
            on("vector", v)

            def s(g, gi=gi, scr_=scr_, m=m, h=h):
                wge(g, "tV", tick_v[gi])
                dma(g, scr_[m, :, 2048 * h:2048 * (h + 1)], stg[gi % 2][:], "qA")
                tick_qa[gi] = cnt["qA"]
            on("gpsimd", s)

    def ph_b(ba, bb):
        yG, gq, sig, yS = [ba[0], ba[1]], [ba[2], ba[3]], [ba[4], ba[5]], \
            [ba[6], ba[7]]
        gin, pst = [bb[0], bb[1]], [bb[2], bb[3]]
        rt = {"v": [0, 0], "q": [0, 0], "qa": [0, 0]}
        for m in range(IC):
            par = m % 2

            def sy(sy, m=m, par=par):
                if m == 0:
                    wall(sy)
                if m >= 2:
                    wge(sy, "tV", rt["v"][par])
                dma(sy, gin[par][:], gscr[m])
                rt["q"][par] = cnt["qS"]
            on("sync", sy)

            def s(s, m=m, par=par):
                wge(s, "qS", rt["q"][par])
                if m >= 2:
                    wge(s, "tV", rt["v"][par])
                inc(s.activation(yG[par][:], gin[par][:], AF.Identity,
                                 scale=SB(S_C1), bias=CCc(C_M)), "tS")
                inc(s.activation(sig[par][:], yG[par][:], AF.Sigmoid,
                                 scale=SB(S_SG), bias=SB(S_NMSG)), "tS")
                inc(s.activation(yS[par][:], sig[par][:], AF.Identity,
                                 scale=CCc(C_65536), bias=CCc(C_M)), "tS")
            on("scalar", s)

            sb3 = cnt["tS"]

            def v(v, m=m, par=par, sb3=sb3):
                wge(v, "tS", sb3 - 2)
                inc(v.tensor_scalar(gq[par][:], yG[par][:], MAGIC, None,
                                    OP.subtract), "tV")
                wge(v, "tS", sb3)
                if m >= 2:
                    wge(v, "qA", rt["qa"][par])
                inc(v.scalar_tensor_tensor(pst[par][:], yS[par][:], MAGIC,
                                           gq[par][:], OP.subtract, OP.mult),
                    "tV")
                v.tensor_reduce(rsm, pst[par][:], AX.X, OP.max,
                                apply_absolute_value=True)
                inc(v.tensor_tensor(maccT, maccT, rsm, OP.max), "tV")
                rt["v"][par] = cnt["tV"]
            on("vector", v)

            def s2(g, m=m, par=par):
                wge(g, "tV", rt["v"][par])
                dma(g, pscr[m], pst[par][:], "qA")
                rt["qa"][par] = cnt["qA"]
            on("gpsimd", s2)

    def ph_c(ba, bb):
        yO, g2q, yU = [ba[0], ba[1]], [ba[2], ba[3]], [ba[4], ba[5]]
        pin, uin, ost = [bb[0], bb[1]], [bb[2], bb[3]], [bb[4], bb[5]]
        rt = {"v": [0, 0], "q": [0, 0], "qa": [0, 0]}
        for m in range(IC):
            par = m % 2

            def sy(sy, m=m, par=par):
                if m == 0:
                    wall(sy)
                if m >= 2:
                    wge(sy, "tV", rt["v"][par])
                dma(sy, pin[par][:], pscr[m])
                dma(sy, uin[par][:], uscr[m])
                rt["q"][par] = cnt["qS"]
            on("sync", sy)

            def s(s, m=m, par=par):
                wge(s, "qS", rt["q"][par])
                if m >= 2:
                    wge(s, "tV", rt["v"][par])
                inc(s.activation(yO[par][:], pin[par][:], AF.Identity,
                                 scale=SB(S_C2), bias=CCc(C_M)), "tS")
                inc(s.activation(yU[par][:], uin[par][:], AF.Identity,
                                 scale=SB(S_C3), bias=CCc(C_M)), "tS")
            on("scalar", s)

            sb2 = cnt["tS"]

            def v(v, m=m, par=par, sb2=sb2):
                wge(v, "tS", sb2 - 1)
                inc(v.tensor_scalar(g2q[par][:], yO[par][:], MAGIC, None,
                                    OP.subtract), "tV")
                wge(v, "tS", sb2)
                if m >= 2:
                    wge(v, "qA", rt["qa"][par])
                inc(v.scalar_tensor_tensor(ost[par][:], yU[par][:], MAGIC,
                                           g2q[par][:], OP.subtract, OP.mult),
                    "tV")
                v.tensor_reduce(rsm, ost[par][:], AX.X, OP.max,
                                apply_absolute_value=True)
                inc(v.tensor_tensor(oaccT, oaccT, rsm, OP.max), "tV")
                rt["v"][par] = cnt["tV"]
            on("vector", v)

            def s2(g, m=m, par=par):
                wge(g, "tV", rt["v"][par])
                for tcq in range(NTC):
                    dma(g, oscr[tcq, m], ost[par][:, 512 * tcq:512 * (tcq + 1)],
                        "qA")
                rt["qa"][par] = cnt["qA"]
            on("gpsimd", s2)

    ptk = {"xa": [0, 0], "xv": [0, 0], "fv": 0}

    def post_chunk(tc, src_res, dst_res, add_mode, do_ln, track, row0=0,
                   qa_w=None):
        r0 = row0 + 512 * tc

        def sy(sy):
            if qa_w is None:
                wge(sy, "tS")
                wge(sy, "qA")
                wge(sy, "tV")
            else:
                wge(sy, "qA", qa_w)
                if ptk["fv"]:
                    wge(sy, "tV", ptk["fv"])
            fv = fscr[r0 // 512].rearrange("c p t -> (c p) t")
            dmaT(sy, LNS["ftm"][:], fv)
        on("sync", sy)

        for u in range(4):
            par = u % 2

            def sy2(sy, u=u, par=par):
                if qa_w is None:
                    if u >= 2:
                        wge(sy, "tV")
                elif ptk["xv"][par]:
                    wge(sy, "tV", ptk["xv"][par])
                dma(sy, LNS["xt"][par][:],
                    src_res[r0 + 128 * u:r0 + 128 * (u + 1), :])
            on("sync", sy2)

            def v(v, u=u, par=par):
                wge(v, "qS")
                if qa_w is None:
                    wge(v, "tS")
                if ptk["xa"][par]:
                    wge(v, "qA", ptk["xa"][par])
                if add_mode == "ffn":
                    i = v.scalar_tensor_tensor(LNS["xr"][par][:],
                                               LNS["ftm"][:, u, :],
                                               SB(S_C5), LNS["xt"][par][:],
                                               OP.mult, OP.add)
                else:
                    i = v.tensor_tensor(LNS["xr"][par][:], LNS["ftm"][:, u, :],
                                        LNS["xt"][par][:], OP.add)
                inc(i, "tV")
                ptk["xv"][par] = cnt["tV"]
            on("vector", v)

            def s(s, u=u, par=par):
                wge(s, "tV")
                dma(s, dst_res[r0 + 128 * u:r0 + 128 * (u + 1), :],
                    LNS["xr"][par][:], "qA")
                ptk["xa"][par] = cnt["qA"]
            on("scalar", s)

            if do_ln:
                ln_tile(par, LNS["xr"][par], r0 + 128 * u, track)
        ptk["fv"] = cnt["tV"]

    def ph_d(wdn_sb, oin, ytmp, src_res, dst_res, do_ln, track):
        tick_t, tick_s, tick_qa = {}, {}, {}
        rt = {"q": [0, 0], "pv": [0, 0], "ps": [0, 0]}
        for tc in range(NTC):
            par = tc % 2

            if tc > 0:
                post_chunk(tc - 1, src_res, dst_res, "ffn", do_ln, track,
                           qa_w=tick_qa[2 * (tc - 1) + 1])

            def sy(sy, tc=tc, par=par):
                if tc == 0:
                    wall(sy)
                if tc >= 2:
                    wge(sy, "tV", rt["pv"][par])
                    wge(sy, "tV", rt["ps"][par])
                for k in range(IC):
                    dma(sy, oin[:, par, k, :],
                        oscr[tc, k])
                rt["q"][par] = cnt["qS"]
            on("sync", sy)

            vb0 = cnt["tV"]

            def s(s, tc=tc, par=par, vb0=vb0):
                wge(s, "qS", rt["q"][par])
                if rt["ps"][1 - par]:
                    wge(s, "tV", rt["ps"][1 - par])
                for half in range(2):
                    if half:
                        wge(s, "tV", vb0 + 1)
                    src = oin[:, par, 12 * half:12 * (half + 1), :]
                    inc(s.activation(ytmp[:].rearrange("p a b -> p (a b)"),
                                     src.rearrange("p a b -> p (a b)"),
                                     AF.Identity,
                                     scale=SB(S_C4), bias=CCc(C_M)), "tS")
            on("scalar", s)

            def vp(v, tc=tc, par=par):
                b = cnt["tS"] - 2
                for half in range(2):
                    wge(v, "tS", b + half + 1)
                    ov = oin[:, par, 12 * half:12 * (half + 1), :]
                    inc(v.tensor_scalar(
                        ov.rearrange("p a b -> p (a b)"),
                        ytmp[:].rearrange("p a b -> p (a b)"),
                        MAGIC, None, OP.subtract), "tV")
                rt["ps"][par] = cnt["tV"]
            on("vector", vp)

            sprep = cnt["tV"]

            for h in range(2):
                gi = 2 * tc + h

                def t(t, tc=tc, par=par, h=h, gi=gi, sprep=sprep):
                    wge(t, "tV", sprep)
                    if gi >= 2:
                        wge(t, "tV", tick_s.get(gi - 2, 0))
                    for k in range(IC):
                        for m4 in range(4):
                            m = 4 * h + m4
                            i = t.matmul(bank(4 * (gi % 2) + m4),
                                         wdn_sb[:, k, 128 * m:128 * (m + 1)],
                                         oin[:, par, k, :],
                                         start=(k == 0), stop=(k == IC - 1),
                                         **MM)
                    inc(i, "tT")
                    tick_t[gi] = cnt["tT"]
                on("tensor", t)

                def v2(v, tc=tc, h=h, gi=gi):
                    wge(v, "tT", tick_t[gi])
                    if gi >= 2:
                        wge(v, "qA", tick_qa.get(gi - 2, 0))
                    half = psum[:, 2048 * (gi % 2):2048 * (gi % 2) + 2048]
                    inc(v.tensor_copy(stg[gi % 2][:], half), "tV")
                    tick_s[gi] = cnt["tV"]
                on("vector", v2)

                def s2(s, tc=tc, h=h, gi=gi):
                    wge(s, "tV", tick_s[gi])
                    for m4 in range(4):
                        m = 4 * h + m4
                        dma(s, fscr[tc, m],
                            stg[gi % 2][:, 512 * m4:512 * (m4 + 1)], "qA")
                    tick_qa[gi] = cnt["qA"]
                on("scalar", s2)

            def vtick(v, par=par):
                wge(v, "tS")
                inc(v.engine_nop(), "tV")
                rt["pv"][par] = cnt["tV"]
            on("vector", vtick)

        post_chunk(NTC - 1, src_res, dst_res, "ffn", do_ln, track,
                   qa_w=tick_qa[2 * (NTC - 1) + 1])

    def ph_attn():
        at = ExitStack()
        z2fm = sbt("z2fm", [128, EC, T], BF16, at)
        vbf = sbt("vbf", [128, 16, E], BF16, at)
        pbf = sbt("pbf", [128, 16, 512], BF16, at)
        pvbf = sbt("pvbf", [128, EC, 512], BF16, at)
        aw = sbt("aw", [128, EC, E], BF16, at)
        scb = sbt("scb", [128, T], F32, at)
        dcol = sbt("dcol", [128, 16], F32, at)
        scm = sbt("scm", [128, 1], F32, at)
        rden = [sbt(f"rden{i}", [128, 512], F32, at) for i in range(2)]
        u2 = sbt("u2", [128, 512], BF16, at)
        u2f = sbt("u2f", [128, 512], F32, at)
        ptmp = sbt("ptmp", [128, 512], F32, at)

        def g0(g):
            wall(g)
            inc(acc_init(g), "tG")
        on("gpsimd", g0)

        def cast_w(src):
            tk = {"g": [0, 0], "q": [0, 0]}
            for k in range(EC):
                par = k % 2

                def sy(sy, k=k, par=par):
                    if k == 0:
                        wge(sy, "tT")
                    if k >= 2:
                        wge(sy, "tG", tk["g"][par])
                    else:
                        wge(sy, "tG")
                    dma(sy, LNS["xt"][par][:], src[128 * k:128 * (k + 1), :])
                    tk["q"][par] = cnt["qS"]
                on("sync", sy)

                def g(g, k=k, par=par):
                    wge(g, "qS", tk["q"][par])
                    if k == 0:
                        wge(g, "tT")
                    inc(g.tensor_copy(aw[:, k, :], LNS["xt"][par][:]), "tG")
                    tk["g"][par] = cnt["tG"]
                on("gpsimd", g)

        for b in range(BPC):
            bo = T * b

            def sy(sy, b=b, bo=bo):
                if b == 0:
                    wall(sy)
                wge(sy, "qA")
                wge(sy, "tV")
                dmaT(sy, z2fm[:], zs[bo:bo + T, :])
            on("sync", sy)

            cast_w(wjvT)
            tick_tv = {}
            for sk in range(16):
                def t(t, sk=sk):
                    if sk == 0:
                        wge(t, "qS")
                        wge(t, "tG")
                        wge(t, "tS")
                        wge(t, "tV")
                    if sk >= 2:
                        wge(t, "tV", tick_tv.get(sk - 2, 0))
                    pb = 4 * (sk % 2)
                    for k in range(EC):
                        lhs = z2fm[:, k, 128 * sk:128 * (sk + 1)]
                        t.matmul(bank(pb), lhs, aw[:, k, 0:512],
                                 start=(k == 0), stop=(k == EC - 1), **MM)
                        i = t.matmul(bank(pb + 1), lhs, aw[:, k, 512:1024],
                                     start=(k == 0), stop=(k == EC - 1), **MM)
                    inc(i, "tT")
                on("tensor", t)

                def s(v, sk=sk):
                    wge(v, "tT")
                    pb = 4 * (sk % 2)
                    inc(v.tensor_copy(vbf[:, sk, :],
                                      psum[:, 512 * pb:512 * pb + 1024]),
                        "tV")
                    tick_tv[sk] = cnt["tV"]
                on("vector", s)

            cast_w(wjT)
            for tq in range(4):
                for j in range(EC):
                    def t(t, tq=tq, j=j):
                        if tq == 0 and j == 0:
                            wge(t, "tS")
                            wge(t, "tG")
                        if j >= 2 or tq > 0:
                            wge(t, "tS")
                        pb = 4 + (j % 2)
                        for k in range(EC):
                            i = t.matmul(bank(pb),
                                         aw[:, k, 128 * j:128 * (j + 1)],
                                         z2fm[:, k, 512 * tq:512 * (tq + 1)],
                                         start=(k == 0), stop=(k == EC - 1),
                                         **MM)
                        inc(i, "tT")
                    on("tensor", t)

                    def s(v, tq=tq, j=j):
                        wge(v, "tT")
                        pb = 4 + (j % 2)
                        v.tensor_copy(u2f[:], bank(pb))
                        inc(v.tensor_tensor(u2[:], u2f[:], bank(pb),
                                            OP.mult), "tV")
                    on("vector", s)

                    def t2(t, tq=tq, j=j):
                        wge(t, "tV")
                        inc(t.matmul(bank(7), ones_bf[:], u2[:],
                                     start=(j == 0), stop=(j == EC - 1),
                                     **MM), "tT")
                    on("tensor", t2)

                def v(v, tq=tq):
                    wge(v, "tT")
                    inc(v.tensor_scalar(scb[:, 512 * tq:512 * (tq + 1)],
                                        bank(7), 1.0 / 32.0, None, OP.mult),
                        "tV")
                on("vector", v)

            def v2(v):
                inc(v.tensor_reduce(scm[:], scb[:], AX.X, OP.max), "tV")
            on("vector", v2)

            for sk in range(16):
                def t(t, sk=sk):
                    wge(t, "tV")
                    inc(t.transpose(psum[:, 3072:3200],
                                    scb[:, 128 * sk:128 * (sk + 1)],
                                    ident[:]), "tT")
                on("tensor", t)

                def v3(v, sk=sk):
                    wge(v, "tT")
                    inc(v.tensor_scalar(dcol[:, sk:sk + 1],
                                        psum[:, 3072:3073], scm[:, 0:1], None,
                                        OP.subtract), "tV")
                on("vector", v3)

            cast_w(woT)
            for tq in range(4):
                ns = 4 * tq + 4
                for sck in range(ns):
                    diag = sck >= 4 * tq

                    def s(s, tq=tq, sck=sck, diag=diag):
                        wge(s, "tV")
                        if tq > 0:
                            wge(s, "tT")
                        dst = ptmp[:] if diag else pbf[:, sck, :]
                        inc(s.activation(dst, scb[:, 512 * tq:512 * (tq + 1)],
                                         AF.Exp, scale=dcol[:, sck:sck + 1]),
                            "tS")
                    on("scalar", s)

                    if diag:
                        def v4(v, tq=tq, sck=sck):
                            wge(v, "tS")
                            inc(v.tensor_tensor(pbf[:, sck, :], ptmp[:],
                                                masks[:, sck - 4 * tq, :],
                                                OP.mult), "tV")
                        on("vector", v4)

                def t(t, tq=tq, ns=ns):
                    wge(t, "tS")
                    wge(t, "tV")
                    for sck in range(ns):
                        i = t.matmul(bank(6), ones_bf[:], pbf[:, sck, :],
                                     start=(sck == 0), stop=(sck == ns - 1),
                                     **MM)
                    inc(i, "tT")
                on("tensor", t)

                def s2(s, tq=tq):
                    wge(s, "tT")
                    s.activation(rden[tq % 2][:], bank(6), AF.Ln)
                    inc(s.activation(rden[tq % 2][:], rden[tq % 2][:],
                                     AF.Exp, scale=CCc(C_NONE)), "tS")
                on("scalar", s2)

                for mE in range(EC):
                    def t2(t, tq=tq, mE=mE, ns=ns):
                        if mE == 0:
                            wge(t, "tS")
                        if mE >= 2:
                            wge(t, "tV")
                        pb = mE % 2
                        for sck in range(ns):
                            i = t.matmul(bank(pb),
                                         vbf[:, sck, 128 * mE:128 * (mE + 1)],
                                         pbf[:, sck, :],
                                         start=(sck == 0), stop=(sck == ns - 1),
                                         **MM)
                        inc(i, "tT")
                    on("tensor", t2)

                    def v5(v, tq=tq, mE=mE):
                        wge(v, "tT")
                        inc(v.tensor_tensor(pvbf[:, mE, :], bank(mE % 2),
                                            rden[tq % 2][:], OP.mult), "tV")
                    on("vector", v5)

                wo_qa = {}
                for mO in range(EC):
                    def t3(t, tq=tq, mO=mO):
                        if mO == 0:
                            wge(t, "tV")
                            wge(t, "tG")
                        if mO >= 2:
                            wge(t, "tV")
                        pb = 2 + mO % 2
                        for k in range(EC):
                            i = t.matmul(bank(pb),
                                         aw[:, k, 128 * mO:128 * (mO + 1)],
                                         pvbf[:, k, :],
                                         start=(k == 0), stop=(k == EC - 1),
                                         **MM)
                        inc(i, "tT")
                    on("tensor", t3)

                    def v6(v, tq=tq, mO=mO):
                        wge(v, "tT")
                        if mO >= 4:
                            wge(v, "qA", wo_qa.get(mO - 4, 0))
                        pb = 2 + mO % 2
                        sl = stg[0][:, 512 * (mO % 4):512 * (mO % 4 + 1)]
                        inc(v.tensor_copy(sl, bank(pb)), "tV")
                    on("vector", v6)

                    def s3(s, b=b, bo=bo, tq=tq, mO=mO):
                        wge(s, "tV")
                        sl = stg[0][:, 512 * (mO % 4):512 * (mO % 4 + 1)]
                        dma(s, fscr[(bo + 512 * tq) // 512, mO],
                            sl, "qA")
                        wo_qa[mO] = cnt["qA"]
                    on("scalar", s3)

                post_chunk(tq, x1s, x2s, "attn", True, True, row0=bo)

        flush()
        at.close()

    open_stacks = []

    def mkstack():
        st = ExitStack()
        open_stacks.append(st)
        return st

    def close(st):
        assert open_stacks and open_stacks[-1] is st
        open_stacks.pop()
        st.close()

    ph_setup()
    flush()

    sA = mkstack()
    wup_sb = sbt("wup_sb", [128, EC, I], BF16, sA)
    wgate_sb = sbt("wgate_sb", [128, EC, I], BF16, sA)
    ln_es = None
    if "LN1" in P:
        ln_es = mkstack()
        ln_alloc(ln_es)
        steps = ph_ln1_steps()
    else:
        steps = None
    if "W" in P:
        ph_w(wup_sb, wgate_sb, side_steps=steps)
        flush()
    elif steps:
        for st_ in steps:
            st_()
    if "LN1" in P:
        if "W" not in P:
            for st_ in (steps or []):
                st_()
        flush()
        close(ln_es)
        barrier(zacc, 1, derive_z)
    if "ZQ1" in P:
        zq_es = mkstack()
        zq_fm = sbt("zq_fm", [128, EC, N], BF16, zq_es)
        zft = [sbt(f"zft{i}", [128, 2048], BF16, zq_es) for i in range(2)]
        zqt = sbt("zqt", [128, 2048], F32, zq_es)
        ph_zq(zq_fm, zft, zqt)
        flush()
    if "A1" in P:
        ph_a(zq_fm, wup_sb, wgate_sb)
        flush()
        close(zq_es)
        close(sA)
        barrier(sm16[:, 4:6], 2, derive_a)
    if "B1" in P:
        bc_es = mkstack()
        ba = [sbt(f"ba{i}", [128, N], F32, bc_es) for i in range(8)]
        bb = [sbt(f"bb{i}", [128, N], BF16, bc_es) for i in range(6)]
        ph_b(ba, bb)
        flush()
        barrier(maccT, 1, derive_p)
    if "C1" in P:
        ph_c(ba, bb)
        flush()
        close(bc_es)
        barrier(oaccT, 1, derive_o)
    if "D1" in P:
        d_es = mkstack()
        ln_alloc(d_es)
        wdn_sb = sbt("wdn_sb", [128, IC, E], BF16, d_es)
        oin = sbt("oin", [128, 2, IC, 512], BF16, d_es)
        ytmp = sbt("ytmp", [128, 12, 512], F32, d_es)

        def syw(sy):
            wge(sy, "qA")
            for k in range(IC):
                dma(sy, wdn_sb[:, k, :], dq["down"][k])
        on("sync", syw)

        ph_d(wdn_sb, oin, ytmp, x0, x1s, True, False)
        flush()
        close(d_es)
    if "ATTN" in P:
        a_es = mkstack()
        ln_alloc(a_es)
        ph_attn()
        flush()
        close(a_es)
        barrier(zacc, 1, derive_z)
    if "ZQ2" in P:
        zq_es = mkstack()
        zq_fm = sbt("zq_fm2", [128, EC, N], BF16, zq_es)
        zft = [sbt(f"zft2_{i}", [128, 2048], BF16, zq_es) for i in range(2)]
        zqt = sbt("zqt2", [128, 2048], F32, zq_es)
        ph_zq(zq_fm, zft, zqt)
        flush()
    if "A2" in P:
        sA2 = mkstack()
        wup2 = sbt("wup2", [128, EC, I], BF16, sA2)
        wgate2 = sbt("wgate2", [128, EC, I], BF16, sA2)

        def syw2(sy):
            wge(sy, "qA")
            for k in range(EC):
                dma(sy, wup2[:, k, :], dq["up"][k])
                dma(sy, wgate2[:, k, :], dq["gate"][k])
        on("sync", syw2)

        ph_a(zq_fm, wup2, wgate2)
        flush()
        close(sA2)
        close(zq_es)
        barrier(sm16[:, 4:6], 2, derive_a)
    if "B2" in P:
        bc_es = mkstack()
        ba = [sbt(f"ba2_{i}", [128, N], F32, bc_es) for i in range(8)]
        bb = [sbt(f"bb2_{i}", [128, N], BF16, bc_es) for i in range(6)]
        ph_b(ba, bb)
        flush()
        barrier(maccT, 1, derive_p)
    if "C2" in P:
        ph_c(ba, bb)
        flush()
        close(bc_es)
        barrier(oaccT, 1, derive_o)
    if "D2" in P:
        d_es = mkstack()
        ln_alloc(d_es)
        wdn_sb = sbt("wdn2", [128, IC, E], BF16, d_es)
        oin = sbt("oin2", [128, 2, IC, 512], BF16, d_es)
        ytmp = sbt("ytmp2", [128, 12, 512], F32, d_es)

        def syw3(sy):
            wge(sy, "qA")
            for k in range(IC):
                dma(sy, wdn_sb[:, k, :], dq["down"][k])
        on("sync", syw3)

        ph_d(wdn_sb, oin, ytmp, x2s, out, False, False)
        flush()
        close(d_es)

    def fin(sy):
        wall(sy)
    on("sync", fin)
    flush()

    while open_stacks:
        open_stacks.pop().close()
    es.close()
    return nc


def _host_consts():
    import ml_dtypes
    bf = ml_dtypes.bfloat16
    ccv = np.zeros((128, 12), np.float32)
    ccv[:, C_M] = MAGIC
    ccv[:, C_NM] = -MAGIC
    ccv[:, C_EPS] = 1e-5
    ccv[:, C_NHALF] = -0.5
    ccv[:, C_NONE] = -1.0
    ccv[:, C_65536] = 65536.0
    ccv[:, C_I1024] = 1.0 / 1024.0
    ccv[:, C_HALF] = 0.5
    ccv[:, 8] = 1.0
    ccv[:, 9] = -1.0
    ccv[:, 10] = 1.0
    ccv[:, 11] = -1.0
    identv = np.eye(128, dtype=np.float32)
    onesv = np.ones((128, 128), dtype=bf)
    tidx = np.arange(512)[None, None, :]
    pidx = np.arange(128)[:, None, None]
    cidx = np.arange(4)[None, :, None]
    maskv = (tidx >= 128 * cidx + pidx).astype(bf)
    col = np.arange(I)
    gv = (np.arange(96)[:, None] ==
          (4 * (col // 128) + (col % 128) // 32)[None, :]).astype(bf)
    return ccv, identv, onesv, maskv, gv


def kernel(L0_attn_x, L0_attn_wj_w, L0_attn_wjv_w, L0_attn_wo_w,
           L0_ffn_w_gate, L0_ffn_w_up, L0_ffn_w_down, _dbg=(), _phase=None):
    x = np.ascontiguousarray(np.asarray(L0_attn_x, np.float32))
    wj = np.ascontiguousarray(np.asarray(L0_attn_wj_w, np.float32))
    wjv = np.ascontiguousarray(np.asarray(L0_attn_wjv_w, np.float32))
    wo = np.ascontiguousarray(np.asarray(L0_attn_wo_w, np.float32))
    wg = np.ascontiguousarray(np.asarray(L0_ffn_w_gate, np.float32))
    wu = np.ascontiguousarray(np.asarray(L0_ffn_w_up, np.float32))
    wd = np.ascontiguousarray(np.asarray(L0_ffn_w_down, np.float32))

    ccv, identv, onesv, maskv, gv = _host_consts()
    common = {
        "wupT": np.ascontiguousarray(wu.T),
        "wgateT": np.ascontiguousarray(wg.T),
        "wdownT": np.ascontiguousarray(wd.T),
        "wupN": wu, "wgateN": wg, "wdownN": wd,
        "wjT": np.ascontiguousarray(wj.T),
        "wjvT": np.ascontiguousarray(wjv.T),
        "woT": np.ascontiguousarray(wo.T),
        "identD": identv, "onesD": onesv, "masksD": maskv,
        "g96D": gv, "constD": ccv,
    }
    in_maps = []
    for c in range(NCORE):
        m = dict(common)
        m["x0"] = np.ascontiguousarray(x[BPC * c: BPC * (c + 1)].reshape(N, E))
        in_maps.append(m)

    nc = build(dbg=_dbg, phase_limit=_phase)
    res = run_bass_kernel_spmd(nc, in_maps, core_ids=list(range(NCORE)))
    if _dbg or _phase is not None:
        return res
    return np.concatenate(
        [r["out"].reshape(BPC, T, E) for r in res.results], axis=0)


# version 22 (final)
# speedup vs baseline: 1.0369x; 1.0369x over previous
import numpy as np
from contextlib import ExitStack

import concourse.bass as bass
import concourse.mybir as mybir
from concourse.bass_utils import run_bass_kernel_spmd

F32, BF16 = mybir.dt.float32, mybir.dt.bfloat16
AF, OP, AX = mybir.ActivationFunctionType, mybir.AluOpType, mybir.AxisListType

MAGIC = float(np.float32(1.5 * 2 ** 23))
NCORE = 8
B, T, E, I = 16, 2048, 1024, 3072
BPC = B // NCORE
N = BPC * T
EC, IC = E // 128, I // 128
NT = N // 128
NTC = N // 512
QM = 32767.0
NEGI, POSI = -3.0e38, 3.0e38

C_M, C_NM, C_EPS, C_NHALF, C_NONE, C_65536, C_I1024, C_HALF = range(8)
S_S1, S_IS1, S_SG, S_C1, S_NMSG, S_C3, S_C2, S_C4, S_C5, S_SU, S_SG2 = range(11)

ALL_PHASES = ("W", "LN1", "ZQ1", "A1", "B1", "C1", "D1", "ATTN",
              "ZQ2", "A2", "B2", "C2", "D2")


def build(dbg=(), phase_limit=None):
    nc = bass.Bass("TRN2", target_bir_lowering=False)
    phases = ALL_PHASES if phase_limit is None else \
        ALL_PHASES[: ALL_PHASES.index(phase_limit) + 1]
    P = set(phases)

    def dram(name, shape, dtype, kind="Internal"):
        if name in dbg and kind == "Internal":
            kind = "ExternalOutput"
        return nc.dram_tensor(name, shape, dtype, kind=kind).ap()

    x0 = dram("x0", [N, E], F32, "ExternalInput")
    wT = {"up": dram("wupT", [E, I], F32, "ExternalInput"),
          "gate": dram("wgateT", [E, I], F32, "ExternalInput"),
          "down": dram("wdownT", [I, E], F32, "ExternalInput")}
    wNat = {"up": dram("wupN", [I, E], F32, "ExternalInput"),
            "gate": dram("wgateN", [I, E], F32, "ExternalInput"),
            "down": dram("wdownN", [E, I], F32, "ExternalInput")}
    wjT = dram("wjT", [E, E], F32, "ExternalInput")
    wjvT = dram("wjvT", [E, E], F32, "ExternalInput")
    woT = dram("woT", [E, E], F32, "ExternalInput")
    identD = dram("identD", [128, 128], F32, "ExternalInput")
    onesD = dram("onesD", [128, 128], BF16, "ExternalInput")
    masksD = dram("masksD", [128, 4, 512], BF16, "ExternalInput")
    g96D = dram("g96D", [96, I], BF16, "ExternalInput")
    constD = dram("constD", [128, 12], F32, "ExternalInput")

    out = dram("out", [N, E], F32, "ExternalOutput")

    x1s = dram("x1s", [N, E], F32)
    x2s = dram("x2s", [N, E], F32)
    zs = dram("zs", [N, E], BF16)
    gscr = dram("gscr", [IC, 128, N], BF16)
    uscr = dram("uscr", [IC, 128, N], BF16)
    pscr = dram("pscr", [IC, 128, N], BF16)
    oscr = dram("oscr", [NTC, IC, 128, 512], BF16)
    fscr = dram("fscr", [NTC, EC, 128, 512], BF16)
    dq = {"up": dram("dqup", [EC, 128, I], BF16),
          "gate": dram("dqgate", [EC, 128, I], BF16),
          "down": dram("dqdown", [IC, 128, E], BF16)}
    sclr = dram("sclr", [1, 16], F32)
    NBAR = 8
    ccin = [dram(f"ccin{k}", [1, 4], F32) for k in range(NBAR)]
    ccout = [dram(f"ccout{k}", [1, 32], F32) for k in range(NBAR)]

    es = ExitStack()

    def sbt(name, shape, dt, st=None):
        return (st or es).enter_context(nc.sbuf_tensor(name, shape, dt))

    cc = sbt("cc", [128, 12], F32)
    ident = sbt("ident", [128, 128], F32)
    ones_bf = sbt("ones_bf", [128, 128], BF16)
    masks = sbt("masks", [128, 4, 512], BF16)
    g96 = sbt("g96", [96, I], BF16)
    sv = sbt("sv", [1, 16], F32)
    sbc = sbt("sbc", [128, 16], F32)
    tm32 = sbt("tm32", [1, 32], F32)
    tm4 = sbt("tm4", [4, 1], F32)
    sm16 = sbt("sm16", [128, 16], F32)
    zacc = sm16[:, 0:1]
    maccT = sm16[:, 1:2]
    oaccT = sm16[:, 2:3]
    rsm = sm16[:, 3:4]
    aacc = sm16[:, 4:8]
    shp4 = sm16[:, 8:12]
    lnm = [sm16[:, 12:13], sm16[:, 13:14]]
    lnv = [sm16[:, 14:15], sm16[:, 15:16]]
    stg = [sbt(f"stg{i}", [128, 2048], BF16) for i in range(2)]

    LNS = {}

    ln_n = [0]

    def ln_alloc(st):
        u = ln_n[0]
        ln_n[0] += 1
        LNS["xt"] = [sbt(f"xt{u}_{i}", [128, E], F32, st) for i in range(2)]
        LNS["xr"] = [sbt(f"xr{u}_{i}", [128, E], F32, st) for i in range(2)]
        LNS["xq"] = [sbt(f"xq{u}_{i}", [128, E], F32, st) for i in range(2)]
        LNS["ztl"] = [sbt(f"ztl{u}_{i}", [128, E], BF16, st) for i in range(2)]
        LNS["ftm"] = sbt(f"ftm{u}", [128, 4, E], BF16, st)

    psum = es.enter_context(nc.psum_tensor("pb", [128, 4096], F32))

    def bank(i):
        return psum[:, 512 * i: 512 * (i + 1)]

    sems, cnt = {}, {}
    for s in ("qS", "qA", "tT", "tV", "tS", "tG", "tC"):
        sems[s] = es.enter_context(nc.semaphore(s))
        cnt[s] = 0

    def inc(inst, s, n=1):
        if s in ("tV", "tS", "tG"):
            return cnt[s]
        inst.then_inc(sems[s], n)
        cnt[s] += n
        return cnt[s]

    def dma(eng, out_, in_, q="qS"):
        return inc(eng.dma_start(out=out_, in_=in_), q, 16)

    def dmaT(eng, out_, in_, q="qS"):
        return inc(eng.dma_start_transpose(out=out_, in_=in_), q, 16)

    def wge(eng, s, v=None):
        eng.wait_ge(sems[s], cnt[s] if v is None else v)

    def wall(eng):
        for s in sems:
            if cnt[s]:
                wge(eng, s)

    blk = es.enter_context(nc.Block())

    pend = {e: [] for e in ("sync", "gpsimd", "tensor", "vector", "scalar")}
    _OWN = {"vector": "tV", "scalar": "tS", "gpsimd": "tG"}
    _NOSELF = ("dma_start", "dma_start_transpose", "collective_compute")

    class FakeInst:
        def __init__(self, item):
            self.item = item

        def then_inc(self, sem, n):
            self.item[2].append((sem, n))
            return self

    class RecEng:
        def __init__(self, name):
            self._n = name
            self._own = _OWN.get(name)

        def wait_ge(self, sem, val):
            pend[self._n].append(("wait", sem, val))

        def _raw(self, closure, selfsync=True):
            incs = []
            if self._own is not None:
                pend[self._n].append(("wait", sems[self._own], cnt[self._own]))
                if selfsync:
                    incs.append((sems[self._own], 1))
                    cnt[self._own] += 1
            item = ["op", closure, incs]
            pend[self._n].append(item)
            return FakeInst(item)

        def __getattr__(self, a):
            if a.startswith("_"):
                raise AttributeError(a)

            def call(*args, **kw):
                return self._raw(
                    lambda E, a=a, args=args, kw=kw: getattr(E, a)(*args, **kw),
                    selfsync=a not in _NOSELF)
            return call

    RECS = {e: RecEng(e) for e in pend}
    GROUPED = ()

    def on(engine, fn):
        fn(RECS[engine])
        flush(only=[e for e in pend if e not in GROUPED])

    def flush(only=None):
        for e in ("sync", "gpsimd", "tensor", "vector", "scalar"):
            if only is not None and e not in only:
                continue
            q = pend[e]
            if not q:
                continue
            pend[e] = []

            def body(E, q=q):
                for it in q:
                    if it[0] == "wait":
                        E.wait_ge(it[1], it[2])
                    else:
                        inst = it[1](E)
                        for sem, n in it[2]:
                            inst.then_inc(sem, n)
            getattr(blk, e)(body)

    def SV(c):
        return sv[0:1, c:c + 1]

    def SB(c):
        return sbc[:, c:c + 1]

    def CCc(c):
        return cc[:, c:c + 1]

    def TM(c):
        return tm32[0:1, c:c + 1]

    MM = dict(skip_group_check=True)

    def acc_init(g):
        g.memset(zacc, 0.0)
        g.memset(maccT, 0.0)
        g.memset(oaccT, 0.0)
        g.memset(sm16[:, 4:5], 0.0)
        return g.memset(sm16[:, 5:6], 0.0)

    def ph_setup():
        def sy(sy):
            dma(sy, cc[:], constD)
            dma(sy, ident[:], identD)
            dma(sy, ones_bf[:], onesD)
            dma(sy, masks[:], masksD)
            dma(sy, g96[:], g96D)
        on("sync", sy)

        def g(g):
            inc(acc_init(g), "tG")
        on("gpsimd", g)

    bar_i = [0]

    def barrier(ship_ap, nval, derive_fn):
        flush()
        k = bar_i[0]
        bar_i[0] += 1

        def t(t):
            wall(t)
            inc(t.transpose(psum[0:nval, 0:128], ship_ap, ident[:]), "tT")
        on("tensor", t)

        def v(v):
            wge(v, "tT")
            inc(v.tensor_reduce(tm4[0:nval, 0:1], psum[0:nval, 0:128],
                                AX.X, OP.max), "tV")
        on("vector", v)

        def sy(sy):
            wge(sy, "tV")
            dma(sy, ccin[k][:, 0:nval], tm4[0:nval, 0:1])
        on("sync", sy)

        def g(g):
            wge(g, "qS")
            inc(g.collective_compute(
                "AllGather", OP.bypass,
                replica_groups=[list(range(NCORE))],
                ins=[ccin[k][:, 0:nval]], outs=[ccout[k][:, 0:8 * nval]]), "tC")
        on("gpsimd", g)

        def sy2(sy):
            wge(sy, "tC")
            dma(sy, tm32[0:1, 0:8 * nval], ccout[k][:, 0:8 * nval])
        on("sync", sy2)

        def v2(v):
            wge(v, "qS")
            for r in range(1, 8):
                v.tensor_tensor(tm32[0:1, 0:nval], tm32[0:1, 0:nval],
                                tm32[0:1, nval * r: nval * (r + 1)], OP.max)
            inc(derive_fn(v), "tV")
        on("vector", v2)

        def sy3(sy):
            wge(sy, "tV")
            dma(sy, sclr, sv[0:1, 0:16])
            wge(sy, "qS")
            dma(sy, sbc[:], sclr[0:1, :].partition_broadcast(128))
        on("sync", sy3)
        flush()

    def derive_z(v):
        v.tensor_scalar(SV(S_S1), TM(0), 1.0 / QM, 1e-12, OP.mult, OP.max)
        return v.reciprocal(SV(S_IS1), SV(S_S1))

    def derive_a(v):
        v.tensor_tensor(TM(8), TM(0), SV(S_S1), OP.mult)
        v.tensor_scalar(TM(8), TM(8), 1.0 / QM, 1e-12, OP.mult, OP.max)
        v.tensor_tensor(TM(9), TM(1), SV(S_S1), OP.mult)
        v.tensor_scalar(TM(9), TM(9), 1.0 / QM, 1e-12, OP.mult, OP.max)
        v.tensor_copy(SV(S_SG), TM(9))
        v.tensor_copy(SV(S_SU), TM(8))
        v.reciprocal(TM(10), TM(8))
        v.reciprocal(TM(11), TM(9))
        v.tensor_tensor(SV(S_C3), SV(S_S1), TM(10), OP.mult)
        v.tensor_tensor(SV(S_C1), SV(S_S1), TM(11), OP.mult)
        return v.tensor_scalar(SV(S_NMSG), SV(S_SG), -MAGIC, None, OP.mult)

    def derive_p(v):
        v.tensor_tensor(TM(1), TM(0), SV(S_SG), OP.mult)
        v.tensor_scalar(TM(1), TM(1), 1.0 / 65536.0, None, OP.mult)
        v.tensor_scalar(TM(2), TM(1), 1.0 / QM, 1e-12, OP.mult, OP.max)
        v.tensor_copy(SV(S_SG2), TM(2))
        v.reciprocal(TM(3), TM(2))
        v.tensor_scalar(TM(4), SV(S_SG), 1.0 / 65536.0, None, OP.mult)
        return v.tensor_tensor(SV(S_C2), TM(4), TM(3), OP.mult)

    def derive_o(v):
        v.tensor_tensor(TM(1), SV(S_SG2), SV(S_SU), OP.mult)
        v.tensor_tensor(TM(2), TM(0), TM(1), OP.mult)
        v.tensor_scalar(TM(3), TM(2), 1.0 / QM, 1e-12, OP.mult, OP.max)
        v.reciprocal(TM(4), TM(3))
        v.tensor_tensor(SV(S_C4), TM(1), TM(4), OP.mult)
        return v.tensor_scalar(SV(S_C5), TM(3), 0.5, None, OP.mult)

    def ph_w(wup_sb, wgate_sb, side_steps=None):
        def side():
            if side_steps:
                side_steps.pop(0)()
        ws = ExitStack()
        smax = sbt("smax", [128, 768], F32, ws)
        srtm = sbt("srtm", [128, 768], F32, ws)
        sTb = sbt("sTb", [96, I], BF16, ws)
        rsTb = sbt("rsTb", [96, I], BF16, ws)
        wtr = sbt("wtr", [128, 2, I], F32, ws)
        dqt = sbt("dqt", [128, 2, I], BF16, ws)
        rtm = sbt("rtm", [128, 512], F32, ws)

        for wname, sdst in (("up", wup_sb), ("gate", wgate_sb), ("down", None)):
            O, In = (I, E) if wname != "down" else (E, I)
            G = In // 32
            noc, nic, nsl = O // 128, In // 128, O // 512

            for oc in range(noc):
                def sy(sy, oc=oc):
                    wge(sy, "tV")
                    if oc == 0:
                        wall(sy)
                    dma(sy, wtr[:, 0, 0:In], wNat[wname][128 * oc:128 * (oc + 1), :])
                on("sync", sy)

                def v(v, oc=oc, G=G):
                    wge(v, "qS")
                    src = wtr[:, 0, 0:In].rearrange("p (g k) -> p g k", k=32)
                    inc(v.tensor_reduce(smax[:, G * oc:G * (oc + 1)], src,
                                        AX.X, OP.max,
                                        apply_absolute_value=True), "tV")
                on("vector", v)
                side()

            def v(v, G=G, noc=noc):
                inc(v.tensor_scalar(srtm[:, 0:G * noc], smax[:, 0:G * noc],
                                    1.0 / 7.0, 1e-12, OP.mult, OP.max), "tV")
            on("vector", v)

            for oc in range(noc):
                def t(t, oc=oc, G=G):
                    wge(t, "tV")
                    inc(t.transpose(psum[0:G, 2048:2176],
                                    srtm[:, G * oc:G * (oc + 1)],
                                    ident[:]), "tT")
                on("tensor", t)

                def v(v, oc=oc, G=G):
                    wge(v, "tT")
                    v.tensor_copy(sTb[0:G, 128 * oc:128 * (oc + 1)],
                                  psum[0:G, 2048:2176])

                    def _rcp(E, G=G, oc=oc):
                        with nc.allow_low_precision(
                                reason="bf16 block-scale expansion operand"):
                            return E.reciprocal(
                                rsTb[0:G, 128 * oc:128 * (oc + 1)],
                                psum[0:G, 2048:2176])
                    v._raw(_rcp)
                on("vector", v)

            rt = {"v": [0, 0], "qa": [0, 0], "q": [0, 0]}
            stt_tick = {}
            gslc = [0]
            for ic in range(nic):
                par = ic % 2

                def sy(sy, ic=ic, par=par, O=O):
                    if ic >= 2:
                        wge(sy, "tV", rt["v"][par])
                    elif ic == 0:
                        wge(sy, "tV")
                    dma(sy, wtr[:, par, 0:O],
                        wT[wname][128 * ic:128 * (ic + 1), :])
                    rt["q"][par] = cnt["qS"]
                on("sync", sy)

                for sl in range(nsl):
                    g_ = gslc[0]
                    gslc[0] += 1
                    pb = 2 * (g_ % 3)

                    def t(t, ic=ic, par=par, sl=sl, G=G, g_=g_, pb=pb):
                        if sl == 0:
                            wge(t, "qS", rt["q"][par])
                            if ic >= 2:
                                wge(t, "qA", rt["qa"][par])
                            elif ic == 0:
                                wge(t, "tV")
                        if stt_tick.get(g_ - 3) is not None:
                            wge(t, "tV", stt_tick[g_ - 3])
                        gsl = g96[0:G, 128 * ic:128 * (ic + 1)]
                        t.matmul(bank(pb), gsl,
                                 sTb[0:G, 512 * sl:512 * (sl + 1)],
                                 start=True, stop=True, **MM)
                        inc(t.matmul(bank(pb + 1), gsl,
                                     rsTb[0:G, 512 * sl:512 * (sl + 1)],
                                     start=True, stop=True, **MM), "tT")
                    on("tensor", t)

                    def v(v, par=par, sl=sl, g_=g_, pb=pb):
                        wge(v, "tT")
                        s_ = slice(512 * sl, 512 * (sl + 1))
                        v.tensor_tensor(rtm[:, 0:512], wtr[:, par, s_],
                                        bank(pb + 1), OP.mult)
                        v.tensor_scalar(rtm[:, 0:512], rtm[:, 0:512], MAGIC,
                                        None, OP.add)
                        inc(v.scalar_tensor_tensor(
                            dqt[:, par, s_], rtm[:, 0:512], MAGIC,
                            bank(pb), OP.subtract, OP.mult), "tV")
                        stt_tick[g_] = cnt["tV"]
                        rt["v"][par] = cnt["tV"]
                    on("vector", v)

                def s2(s, ic=ic, par=par, wname=wname, sdst=sdst, O=O):
                    wge(s, "tV", rt["v"][par])
                    dma(s, dq[wname][ic], dqt[:, par, 0:O], "qA")
                    rt["qa"][par] = cnt["qA"]
                    if sdst is not None:
                        inc(s.copy(sdst[:, ic, 0:O], dqt[:, par, 0:O]), "tS")
                on("scalar", s2)
                side()

            def vfin(v):
                wge(v, "tS")
                wge(v, "qA", max(rt["qa"]))
                inc(v.engine_nop(), "tV")
            on("vector", vfin)

        ws.close()

    lntk = {"z": [0, 0]}

    def ln_tile(par, src_tile, zdst_row, track):
        def v(v):
            inc(v.tensor_reduce(lnm[par], src_tile[:], AX.X, OP.add), "tV")
        on("vector", v)

        def v2(v):
            v.tensor_tensor(LNS["xq"][par][:], src_tile[:], src_tile[:],
                            OP.mult)
            v.tensor_reduce(lnv[par], LNS["xq"][par][:], AX.X, OP.add)
            v.tensor_scalar(lnm[par], lnm[par], 1.0 / E, None, OP.mult)
            v.tensor_tensor(rsm, lnm[par], lnm[par], OP.mult)
            inc(v.scalar_tensor_tensor(lnv[par], lnv[par], 1.0 / E,
                                       rsm, OP.mult, OP.subtract), "tV")
        on("vector", v2)

        def s2(s):
            wge(s, "tV")
            s.activation(lnv[par], lnv[par], AF.Ln, bias=CCc(C_EPS))
            inc(s.activation(lnv[par], lnv[par], AF.Exp,
                             scale=CCc(C_NHALF)), "tS")
        on("scalar", s2)

        def v3(v):
            wge(v, "tS")
            if lntk["z"][par]:
                wge(v, "qA", lntk["z"][par])
            i = v.tensor_scalar(LNS["ztl"][par][:], src_tile[:], lnm[par],
                                lnv[par], OP.subtract, OP.mult)
            if track:
                v.tensor_reduce(rsm, LNS["ztl"][par][:], AX.X, OP.max,
                                apply_absolute_value=True)
                i = v.tensor_tensor(zacc, zacc, rsm, OP.max)
            inc(i, "tV")
        on("vector", v3)

        def s3(s):
            wge(s, "tV")
            dma(s, zs[zdst_row:zdst_row + 128, :], LNS["ztl"][par][:], "qA")
            lntk["z"][par] = cnt["qA"]
        on("scalar", s3)

    def ph_ln1_steps():
        rt = {"v": [0, 0]}
        steps = []

        def step(j):
            par = j % 2

            def sy(sy, j=j, par=par):
                if j >= 2:
                    wge(sy, "tV", rt["v"][par])
                dma(sy, LNS["xt"][par][:], x0[128 * j:128 * (j + 1), :])
            on("sync", sy)

            def v0(v):
                wge(v, "qS")
                inc(v.engine_nop(), "tV")
            on("vector", v0)

            ln_tile(par, LNS["xt"][par], 128 * j, True)
            rt["v"][par] = cnt["tV"]
        return [lambda j=j: step(j) for j in range(NT)]

    def ph_zq(zq_fm, zft, zqt):
        rt = {"v": [0, 0]}
        for c in range(EC):
            for h in range(2):
                idx = 2 * c + h
                par = idx % 2

                def sy(sy, c=c, h=h, par=par, idx=idx):
                    if idx == 0:
                        wall(sy)
                    if idx >= 2:
                        wge(sy, "tV", rt["v"][par])
                    zin = zs[2048 * h:2048 * (h + 1), 128 * c:128 * (c + 1)]
                    dmaT(sy, zft[par][:], zin)
                on("sync", sy)

                def v(v, c=c, h=h, par=par):
                    wge(v, "qS")
                    v.tensor_scalar(zqt[:], zft[par][:], SB(S_IS1),
                                    MAGIC, OP.mult, OP.add)
                    inc(v.tensor_scalar(zq_fm[:, c, 2048 * h:2048 * (h + 1)],
                                        zqt[:], MAGIC, None, OP.subtract),
                        "tV")
                    rt["v"][par] = cnt["tV"]
                on("vector", v)

    def ph_a(zq_fm, wsb_up, wsb_gate):
        groups = []
        for wname, wsb, scr_, cmax, cmin in (
                ("up", wsb_up, uscr, 4, 0), ("gate", wsb_gate, gscr, 5, 0)):
            for m in range(IC):
                for h in range(2):
                    groups.append((wsb, scr_, cmax, cmin, m, h))
        tick_t, tick_s, tick_v, tick_qa = {}, {}, {}, {}
        for gi, (wsb, scr_, cmax, cmin, m, h) in enumerate(groups):
            def t(t, gi=gi, wsb=wsb, m=m, h=h):
                if gi == 0:
                    wall(t)
                if gi >= 2:
                    wge(t, "tV", tick_v[gi - 2])
                for k in range(EC):
                    lhs = wsb[:, k, 128 * m:128 * (m + 1)]
                    for tc4 in range(4):
                        tcg = 4 * h + tc4
                        i = t.matmul(bank(4 * (gi % 2) + tc4), lhs,
                                     zq_fm[:, k, 512 * tcg:512 * (tcg + 1)],
                                     start=(k == 0), stop=(k == EC - 1), **MM)
                inc(i, "tT")
                tick_t[gi] = cnt["tT"]
            on("tensor", t)

            def v(v, gi=gi, cmax=cmax, cmin=cmin):
                wge(v, "tT", tick_t[gi])
                if gi >= 2:
                    wge(v, "qA", tick_qa[gi - 2])
                half = psum[:, 2048 * (gi % 2):2048 * (gi % 2) + 2048]
                v.tensor_copy(stg[gi % 2][:], half)
                v.tensor_reduce(rsm, half, AX.X, OP.max,
                                apply_absolute_value=True)
                inc(v.tensor_tensor(sm16[:, cmax:cmax + 1],
                                    sm16[:, cmax:cmax + 1], rsm, OP.max),
                    "tV")
                tick_v[gi] = cnt["tV"]
                tick_s[gi] = cnt["tV"]
            on("vector", v)

            def s(g, gi=gi, scr_=scr_, m=m, h=h):
                wge(g, "tV", tick_v[gi])
                dma(g, scr_[m, :, 2048 * h:2048 * (h + 1)], stg[gi % 2][:], "qA")
                tick_qa[gi] = cnt["qA"]
            on("gpsimd", s)

    def ph_b(ba, bb):
        yG, gq, sig, yS = [ba[0], ba[1]], [ba[2], ba[3]], [ba[4], ba[5]], \
            [ba[6], ba[7]]
        gin, pst = [bb[0], bb[1]], [bb[2], bb[3]]
        rt = {"v": [0, 0], "q": [0, 0], "qa": [0, 0]}
        for m in range(IC):
            par = m % 2

            def sy(sy, m=m, par=par):
                if m == 0:
                    wall(sy)
                if m >= 2:
                    wge(sy, "tV", rt["v"][par])
                dma(sy, gin[par][:], gscr[m])
                rt["q"][par] = cnt["qS"]
            on("sync", sy)

            def s(s, m=m, par=par):
                wge(s, "qS", rt["q"][par])
                if m >= 2:
                    wge(s, "tV", rt["v"][par])
                inc(s.activation(yG[par][:], gin[par][:], AF.Identity,
                                 scale=SB(S_C1), bias=CCc(C_M)), "tS")
                inc(s.activation(sig[par][:], yG[par][:], AF.Sigmoid,
                                 scale=SB(S_SG), bias=SB(S_NMSG)), "tS")
                inc(s.activation(yS[par][:], sig[par][:], AF.Identity,
                                 scale=CCc(C_65536), bias=CCc(C_M)), "tS")
            on("scalar", s)

            sb3 = cnt["tS"]

            def v(v, m=m, par=par, sb3=sb3):
                wge(v, "tS", sb3 - 2)
                inc(v.tensor_scalar(gq[par][:], yG[par][:], MAGIC, None,
                                    OP.subtract), "tV")
                wge(v, "tS", sb3)
                if m >= 2:
                    wge(v, "qA", rt["qa"][par])
                inc(v.scalar_tensor_tensor(pst[par][:], yS[par][:], MAGIC,
                                           gq[par][:], OP.subtract, OP.mult),
                    "tV")
                v.tensor_reduce(rsm, pst[par][:], AX.X, OP.max,
                                apply_absolute_value=True)
                inc(v.tensor_tensor(maccT, maccT, rsm, OP.max), "tV")
                rt["v"][par] = cnt["tV"]
            on("vector", v)

            def s2(g, m=m, par=par):
                wge(g, "tV", rt["v"][par])
                dma(g, pscr[m], pst[par][:], "qA")
                rt["qa"][par] = cnt["qA"]
            on("gpsimd", s2)

    def ph_c(ba, bb):
        yO, g2q, yU = [ba[0], ba[1]], [ba[2], ba[3]], [ba[4], ba[5]]
        pin, uin, ost = [bb[0], bb[1]], [bb[2], bb[3]], [bb[4], bb[5]]
        rt = {"v": [0, 0], "q": [0, 0], "qa": [0, 0]}
        for m in range(IC):
            par = m % 2

            def sy(sy, m=m, par=par):
                if m == 0:
                    wall(sy)
                if m >= 2:
                    wge(sy, "tV", rt["v"][par])
                dma(sy, pin[par][:], pscr[m])
                dma(sy, uin[par][:], uscr[m])
                rt["q"][par] = cnt["qS"]
            on("sync", sy)

            def s(s, m=m, par=par):
                wge(s, "qS", rt["q"][par])
                if m >= 2:
                    wge(s, "tV", rt["v"][par])
                inc(s.activation(yO[par][:], pin[par][:], AF.Identity,
                                 scale=SB(S_C2), bias=CCc(C_M)), "tS")
                inc(s.activation(yU[par][:], uin[par][:], AF.Identity,
                                 scale=SB(S_C3), bias=CCc(C_M)), "tS")
            on("scalar", s)

            sb2 = cnt["tS"]

            def v(v, m=m, par=par, sb2=sb2):
                wge(v, "tS", sb2 - 1)
                inc(v.tensor_scalar(g2q[par][:], yO[par][:], MAGIC, None,
                                    OP.subtract), "tV")
                wge(v, "tS", sb2)
                if m >= 2:
                    wge(v, "qA", rt["qa"][par])
                inc(v.scalar_tensor_tensor(ost[par][:], yU[par][:], MAGIC,
                                           g2q[par][:], OP.subtract, OP.mult),
                    "tV")
                v.tensor_reduce(rsm, ost[par][:], AX.X, OP.max,
                                apply_absolute_value=True)
                inc(v.tensor_tensor(oaccT, oaccT, rsm, OP.max), "tV")
                rt["v"][par] = cnt["tV"]
            on("vector", v)

            def s2(g, m=m, par=par):
                wge(g, "tV", rt["v"][par])
                for tcq in range(NTC):
                    dma(g, oscr[tcq, m], ost[par][:, 512 * tcq:512 * (tcq + 1)],
                        "qA")
                rt["qa"][par] = cnt["qA"]
            on("gpsimd", s2)

    ptk = {"xa": [0, 0], "xv": [0, 0], "fv": 0}

    def post_chunk(tc, src_res, dst_res, add_mode, do_ln, track, row0=0,
                   qa_w=None):
        r0 = row0 + 512 * tc

        def sy(sy):
            if qa_w is None:
                wge(sy, "tS")
                wge(sy, "qA")
                wge(sy, "tV")
            else:
                wge(sy, "qA", qa_w)
                if ptk["fv"]:
                    wge(sy, "tV", ptk["fv"])
            fv = fscr[r0 // 512].rearrange("c p t -> (c p) t")
            dmaT(sy, LNS["ftm"][:], fv)
        on("sync", sy)

        for u in range(4):
            par = u % 2

            def sy2(sy, u=u, par=par):
                if qa_w is None:
                    if u >= 2:
                        wge(sy, "tV")
                elif ptk["xv"][par]:
                    wge(sy, "tV", ptk["xv"][par])
                dma(sy, LNS["xt"][par][:],
                    src_res[r0 + 128 * u:r0 + 128 * (u + 1), :])
            on("sync", sy2)

            def v(v, u=u, par=par):
                wge(v, "qS")
                if qa_w is None:
                    wge(v, "tS")
                if ptk["xa"][par]:
                    wge(v, "qA", ptk["xa"][par])
                if add_mode == "ffn":
                    i = v.scalar_tensor_tensor(LNS["xr"][par][:],
                                               LNS["ftm"][:, u, :],
                                               SB(S_C5), LNS["xt"][par][:],
                                               OP.mult, OP.add)
                else:
                    i = v.tensor_tensor(LNS["xr"][par][:], LNS["ftm"][:, u, :],
                                        LNS["xt"][par][:], OP.add)
                inc(i, "tV")
                ptk["xv"][par] = cnt["tV"]
            on("vector", v)

            def s(s, u=u, par=par):
                wge(s, "tV")
                dma(s, dst_res[r0 + 128 * u:r0 + 128 * (u + 1), :],
                    LNS["xr"][par][:], "qA")
                ptk["xa"][par] = cnt["qA"]
            on("scalar", s)

            if do_ln:
                ln_tile(par, LNS["xr"][par], r0 + 128 * u, track)
        ptk["fv"] = cnt["tV"]

    def ph_d(wdn_sb, oin, ytmp, src_res, dst_res, do_ln, track):
        tick_t, tick_s, tick_qa = {}, {}, {}
        rt = {"q": [0, 0], "pv": [0, 0], "ps": [0, 0]}
        for tc in range(NTC):
            par = tc % 2

            def sy(sy, tc=tc, par=par):
                if tc == 0:
                    wall(sy)
                if tc >= 2:
                    wge(sy, "tV", rt["pv"][par])
                    wge(sy, "tV", rt["ps"][par])
                for k in range(IC):
                    dma(sy, oin[:, par, k, :],
                        oscr[tc, k])
                rt["q"][par] = cnt["qS"]
            on("sync", sy)

            vb0 = cnt["tV"]

            def s(s, tc=tc, par=par, vb0=vb0):
                wge(s, "qS", rt["q"][par])
                for half in range(2):
                    if half:
                        wge(s, "tV", vb0 + 1)
                    for kk in range(12):
                        k = 12 * half + kk
                        inc(s.activation(ytmp[:, kk, :], oin[:, par, k, :],
                                         AF.Identity,
                                         scale=SB(S_C4), bias=CCc(C_M)), "tS")
            on("scalar", s)

            def vp(v, tc=tc, par=par):
                b = cnt["tS"] - IC
                for half in range(2):
                    wge(v, "tS", b + 12 * (half + 1))
                    ov = oin[:, par, 12 * half:12 * (half + 1), :]
                    inc(v.tensor_scalar(
                        ov.rearrange("p a b -> p (a b)"),
                        ytmp[:].rearrange("p a b -> p (a b)"),
                        MAGIC, None, OP.subtract), "tV")
                rt["ps"][par] = cnt["tV"]
            on("vector", vp)

            sprep = cnt["tV"]

            for h in range(2):
                gi = 2 * tc + h

                def t(t, tc=tc, par=par, h=h, gi=gi, sprep=sprep):
                    wge(t, "tV", sprep - 1)
                    if gi >= 2:
                        wge(t, "tV", tick_s.get(gi - 2, 0))
                    for k in range(IC):
                        if k == 12:
                            wge(t, "tV", sprep)
                        for m4 in range(4):
                            m = 4 * h + m4
                            i = t.matmul(bank(4 * (gi % 2) + m4),
                                         wdn_sb[:, k, 128 * m:128 * (m + 1)],
                                         oin[:, par, k, :],
                                         start=(k == 0), stop=(k == IC - 1),
                                         **MM)
                    inc(i, "tT")
                    tick_t[gi] = cnt["tT"]
                on("tensor", t)

                def v2(v, tc=tc, h=h, gi=gi):
                    wge(v, "tT", tick_t[gi])
                    if gi >= 2:
                        wge(v, "qA", tick_qa.get(gi - 2, 0))
                    half = psum[:, 2048 * (gi % 2):2048 * (gi % 2) + 2048]
                    inc(v.tensor_copy(stg[gi % 2][:], half), "tV")
                    tick_s[gi] = cnt["tV"]
                on("vector", v2)

                def s2(s, tc=tc, h=h, gi=gi):
                    wge(s, "tV", tick_s[gi])
                    for m4 in range(4):
                        m = 4 * h + m4
                        dma(s, fscr[tc, m],
                            stg[gi % 2][:, 512 * m4:512 * (m4 + 1)], "qA")
                    tick_qa[gi] = cnt["qA"]
                on("scalar", s2)

            def vtick(v, par=par):
                wge(v, "tS")
                inc(v.engine_nop(), "tV")
                rt["pv"][par] = cnt["tV"]
            on("vector", vtick)

            if tc > 0:
                post_chunk(tc - 1, src_res, dst_res, "ffn", do_ln, track,
                           qa_w=tick_qa[2 * (tc - 1) + 1])
        post_chunk(NTC - 1, src_res, dst_res, "ffn", do_ln, track,
                   qa_w=tick_qa[2 * (NTC - 1) + 1])

    def ph_attn():
        at = ExitStack()
        z2fm = sbt("z2fm", [128, EC, T], BF16, at)
        vbf = sbt("vbf", [128, 16, E], BF16, at)
        pbf = sbt("pbf", [128, 16, 512], BF16, at)
        pvbf = sbt("pvbf", [128, EC, 512], BF16, at)
        aw = sbt("aw", [128, EC, E], BF16, at)
        scb = sbt("scb", [128, T], F32, at)
        dcol = sbt("dcol", [128, 16], F32, at)
        scm = sbt("scm", [128, 1], F32, at)
        rden = [sbt(f"rden{i}", [128, 512], F32, at) for i in range(2)]
        u2 = sbt("u2", [128, 512], BF16, at)
        u2f = sbt("u2f", [128, 512], F32, at)
        ptmp = sbt("ptmp", [128, 512], F32, at)

        def g0(g):
            wall(g)
            inc(acc_init(g), "tG")
        on("gpsimd", g0)

        def cast_w(src):
            tk = {"g": [0, 0], "q": [0, 0]}
            for k in range(EC):
                par = k % 2

                def sy(sy, k=k, par=par):
                    if k == 0:
                        wge(sy, "tT")
                    if k >= 2:
                        wge(sy, "tG", tk["g"][par])
                    else:
                        wge(sy, "tG")
                    dma(sy, LNS["xt"][par][:], src[128 * k:128 * (k + 1), :])
                    tk["q"][par] = cnt["qS"]
                on("sync", sy)

                def g(g, k=k, par=par):
                    wge(g, "qS", tk["q"][par])
                    if k == 0:
                        wge(g, "tT")
                    inc(g.tensor_copy(aw[:, k, :], LNS["xt"][par][:]), "tG")
                    tk["g"][par] = cnt["tG"]
                on("gpsimd", g)

        for b in range(BPC):
            bo = T * b

            def sy(sy, b=b, bo=bo):
                if b == 0:
                    wall(sy)
                wge(sy, "qA")
                wge(sy, "tV")
                dmaT(sy, z2fm[:], zs[bo:bo + T, :])
            on("sync", sy)

            cast_w(wjvT)
            tick_tv = {}
            for sk in range(16):
                def t(t, sk=sk):
                    if sk == 0:
                        wge(t, "qS")
                        wge(t, "tG")
                        wge(t, "tS")
                        wge(t, "tV")
                    if sk >= 2:
                        wge(t, "tV", tick_tv.get(sk - 2, 0))
                    pb = 4 * (sk % 2)
                    for k in range(EC):
                        lhs = z2fm[:, k, 128 * sk:128 * (sk + 1)]
                        t.matmul(bank(pb), lhs, aw[:, k, 0:512],
                                 start=(k == 0), stop=(k == EC - 1), **MM)
                        i = t.matmul(bank(pb + 1), lhs, aw[:, k, 512:1024],
                                     start=(k == 0), stop=(k == EC - 1), **MM)
                    inc(i, "tT")
                on("tensor", t)

                def s(v, sk=sk):
                    wge(v, "tT")
                    pb = 4 * (sk % 2)
                    inc(v.tensor_copy(vbf[:, sk, :],
                                      psum[:, 512 * pb:512 * pb + 1024]),
                        "tV")
                    tick_tv[sk] = cnt["tV"]
                on("vector", s)

            cast_w(wjT)
            for tq in range(4):
                for j in range(EC):
                    def t(t, tq=tq, j=j):
                        if tq == 0 and j == 0:
                            wge(t, "tS")
                            wge(t, "tG")
                        if j >= 2 or tq > 0:
                            wge(t, "tS")
                        pb = 4 + (j % 2)
                        for k in range(EC):
                            i = t.matmul(bank(pb),
                                         aw[:, k, 128 * j:128 * (j + 1)],
                                         z2fm[:, k, 512 * tq:512 * (tq + 1)],
                                         start=(k == 0), stop=(k == EC - 1),
                                         **MM)
                        inc(i, "tT")
                    on("tensor", t)

                    def s(v, tq=tq, j=j):
                        wge(v, "tT")
                        pb = 4 + (j % 2)
                        v.tensor_copy(u2f[:], bank(pb))
                        inc(v.tensor_tensor(u2[:], u2f[:], bank(pb),
                                            OP.mult), "tV")
                    on("vector", s)

                    def t2(t, tq=tq, j=j):
                        wge(t, "tV")
                        inc(t.matmul(bank(7), ones_bf[:], u2[:],
                                     start=(j == 0), stop=(j == EC - 1),
                                     **MM), "tT")
                    on("tensor", t2)

                def v(v, tq=tq):
                    wge(v, "tT")
                    inc(v.tensor_scalar(scb[:, 512 * tq:512 * (tq + 1)],
                                        bank(7), 1.0 / 32.0, None, OP.mult),
                        "tV")
                on("vector", v)

            def v2(v):
                inc(v.tensor_reduce(scm[:], scb[:], AX.X, OP.max), "tV")
            on("vector", v2)

            for sk in range(16):
                def t(t, sk=sk):
                    wge(t, "tV")
                    inc(t.transpose(psum[:, 3072:3200],
                                    scb[:, 128 * sk:128 * (sk + 1)],
                                    ident[:]), "tT")
                on("tensor", t)

                def v3(v, sk=sk):
                    wge(v, "tT")
                    inc(v.tensor_scalar(dcol[:, sk:sk + 1],
                                        psum[:, 3072:3073], scm[:, 0:1], None,
                                        OP.subtract), "tV")
                on("vector", v3)

            cast_w(woT)
            for tq in range(4):
                ns = 4 * tq + 4
                for sck in range(ns):
                    diag = sck >= 4 * tq

                    def s(s, tq=tq, sck=sck, diag=diag):
                        wge(s, "tV")
                        if tq > 0:
                            wge(s, "tT")
                        dst = ptmp[:] if diag else pbf[:, sck, :]
                        inc(s.activation(dst, scb[:, 512 * tq:512 * (tq + 1)],
                                         AF.Exp, scale=dcol[:, sck:sck + 1]),
                            "tS")
                    on("scalar", s)

                    if diag:
                        def v4(v, tq=tq, sck=sck):
                            wge(v, "tS")
                            inc(v.tensor_tensor(pbf[:, sck, :], ptmp[:],
                                                masks[:, sck - 4 * tq, :],
                                                OP.mult), "tV")
                        on("vector", v4)

                def t(t, tq=tq, ns=ns):
                    wge(t, "tS")
                    wge(t, "tV")
                    for sck in range(ns):
                        i = t.matmul(bank(6), ones_bf[:], pbf[:, sck, :],
                                     start=(sck == 0), stop=(sck == ns - 1),
                                     **MM)
                    inc(i, "tT")
                on("tensor", t)

                def s2(s, tq=tq):
                    wge(s, "tT")
                    s.activation(rden[tq % 2][:], bank(6), AF.Ln)
                    inc(s.activation(rden[tq % 2][:], rden[tq % 2][:],
                                     AF.Exp, scale=CCc(C_NONE)), "tS")
                on("scalar", s2)

                for mE in range(EC):
                    def t2(t, tq=tq, mE=mE, ns=ns):
                        if mE == 0:
                            wge(t, "tS")
                        if mE >= 2:
                            wge(t, "tV")
                        pb = mE % 2
                        for sck in range(ns):
                            i = t.matmul(bank(pb),
                                         vbf[:, sck, 128 * mE:128 * (mE + 1)],
                                         pbf[:, sck, :],
                                         start=(sck == 0), stop=(sck == ns - 1),
                                         **MM)
                        inc(i, "tT")
                    on("tensor", t2)

                    def v5(v, tq=tq, mE=mE):
                        wge(v, "tT")
                        inc(v.tensor_tensor(pvbf[:, mE, :], bank(mE % 2),
                                            rden[tq % 2][:], OP.mult), "tV")
                    on("vector", v5)

                wo_qa = {}
                for mO in range(EC):
                    def t3(t, tq=tq, mO=mO):
                        if mO == 0:
                            wge(t, "tV")
                            wge(t, "tG")
                        if mO >= 2:
                            wge(t, "tV")
                        pb = 2 + mO % 2
                        for k in range(EC):
                            i = t.matmul(bank(pb),
                                         aw[:, k, 128 * mO:128 * (mO + 1)],
                                         pvbf[:, k, :],
                                         start=(k == 0), stop=(k == EC - 1),
                                         **MM)
                        inc(i, "tT")
                    on("tensor", t3)

                    def v6(v, tq=tq, mO=mO):
                        wge(v, "tT")
                        if mO >= 4:
                            wge(v, "qA", wo_qa.get(mO - 4, 0))
                        pb = 2 + mO % 2
                        sl = stg[0][:, 512 * (mO % 4):512 * (mO % 4 + 1)]
                        inc(v.tensor_copy(sl, bank(pb)), "tV")
                    on("vector", v6)

                    def s3(s, b=b, bo=bo, tq=tq, mO=mO):
                        wge(s, "tV")
                        sl = stg[0][:, 512 * (mO % 4):512 * (mO % 4 + 1)]
                        dma(s, fscr[(bo + 512 * tq) // 512, mO],
                            sl, "qA")
                        wo_qa[mO] = cnt["qA"]
                    on("scalar", s3)

                post_chunk(tq, x1s, x2s, "attn", True, True, row0=bo)

        flush()
        at.close()

    open_stacks = []

    def mkstack():
        st = ExitStack()
        open_stacks.append(st)
        return st

    def close(st):
        assert open_stacks and open_stacks[-1] is st
        open_stacks.pop()
        st.close()

    ph_setup()
    flush()

    sA = mkstack()
    wup_sb = sbt("wup_sb", [128, EC, I], BF16, sA)
    wgate_sb = sbt("wgate_sb", [128, EC, I], BF16, sA)
    ln_es = None
    if "LN1" in P:
        ln_es = mkstack()
        ln_alloc(ln_es)
        steps = ph_ln1_steps()
    else:
        steps = None
    if "W" in P:
        ph_w(wup_sb, wgate_sb, side_steps=steps)
        flush()
    elif steps:
        for st_ in steps:
            st_()
    if "LN1" in P:
        if "W" not in P:
            for st_ in (steps or []):
                st_()
        flush()
        close(ln_es)
        barrier(zacc, 1, derive_z)
    if "ZQ1" in P:
        zq_es = mkstack()
        zq_fm = sbt("zq_fm", [128, EC, N], BF16, zq_es)
        zft = [sbt(f"zft{i}", [128, 2048], BF16, zq_es) for i in range(2)]
        zqt = sbt("zqt", [128, 2048], F32, zq_es)
        ph_zq(zq_fm, zft, zqt)
        flush()
    if "A1" in P:
        ph_a(zq_fm, wup_sb, wgate_sb)
        flush()
        close(zq_es)
        close(sA)
        barrier(sm16[:, 4:6], 2, derive_a)
    if "B1" in P:
        bc_es = mkstack()
        ba = [sbt(f"ba{i}", [128, N], F32, bc_es) for i in range(8)]
        bb = [sbt(f"bb{i}", [128, N], BF16, bc_es) for i in range(6)]
        ph_b(ba, bb)
        flush()
        barrier(maccT, 1, derive_p)
    if "C1" in P:
        ph_c(ba, bb)
        flush()
        close(bc_es)
        barrier(oaccT, 1, derive_o)
    if "D1" in P:
        d_es = mkstack()
        ln_alloc(d_es)
        wdn_sb = sbt("wdn_sb", [128, IC, E], BF16, d_es)
        oin = sbt("oin", [128, 2, IC, 512], BF16, d_es)
        ytmp = sbt("ytmp", [128, 12, 512], F32, d_es)

        def syw(sy):
            wge(sy, "qA")
            for k in range(IC):
                dma(sy, wdn_sb[:, k, :], dq["down"][k])
        on("sync", syw)

        ph_d(wdn_sb, oin, ytmp, x0, x1s, True, False)
        flush()
        close(d_es)
    if "ATTN" in P:
        a_es = mkstack()
        ln_alloc(a_es)
        ph_attn()
        flush()
        close(a_es)
        barrier(zacc, 1, derive_z)
    if "ZQ2" in P:
        zq_es = mkstack()
        zq_fm = sbt("zq_fm2", [128, EC, N], BF16, zq_es)
        zft = [sbt(f"zft2_{i}", [128, 2048], BF16, zq_es) for i in range(2)]
        zqt = sbt("zqt2", [128, 2048], F32, zq_es)
        ph_zq(zq_fm, zft, zqt)
        flush()
    if "A2" in P:
        sA2 = mkstack()
        wup2 = sbt("wup2", [128, EC, I], BF16, sA2)
        wgate2 = sbt("wgate2", [128, EC, I], BF16, sA2)

        def syw2(sy):
            wge(sy, "qA")
            for k in range(EC):
                dma(sy, wup2[:, k, :], dq["up"][k])
                dma(sy, wgate2[:, k, :], dq["gate"][k])
        on("sync", syw2)

        ph_a(zq_fm, wup2, wgate2)
        flush()
        close(sA2)
        close(zq_es)
        barrier(sm16[:, 4:6], 2, derive_a)
    if "B2" in P:
        bc_es = mkstack()
        ba = [sbt(f"ba2_{i}", [128, N], F32, bc_es) for i in range(8)]
        bb = [sbt(f"bb2_{i}", [128, N], BF16, bc_es) for i in range(6)]
        ph_b(ba, bb)
        flush()
        barrier(maccT, 1, derive_p)
    if "C2" in P:
        ph_c(ba, bb)
        flush()
        close(bc_es)
        barrier(oaccT, 1, derive_o)
    if "D2" in P:
        d_es = mkstack()
        ln_alloc(d_es)
        wdn_sb = sbt("wdn2", [128, IC, E], BF16, d_es)
        oin = sbt("oin2", [128, 2, IC, 512], BF16, d_es)
        ytmp = sbt("ytmp2", [128, 12, 512], F32, d_es)

        def syw3(sy):
            wge(sy, "qA")
            for k in range(IC):
                dma(sy, wdn_sb[:, k, :], dq["down"][k])
        on("sync", syw3)

        ph_d(wdn_sb, oin, ytmp, x2s, out, False, False)
        flush()
        close(d_es)

    def fin(sy):
        wall(sy)
    on("sync", fin)
    flush()

    while open_stacks:
        open_stacks.pop().close()
    es.close()
    return nc


def _host_consts():
    import ml_dtypes
    bf = ml_dtypes.bfloat16
    ccv = np.zeros((128, 12), np.float32)
    ccv[:, C_M] = MAGIC
    ccv[:, C_NM] = -MAGIC
    ccv[:, C_EPS] = 1e-5
    ccv[:, C_NHALF] = -0.5
    ccv[:, C_NONE] = -1.0
    ccv[:, C_65536] = 65536.0
    ccv[:, C_I1024] = 1.0 / 1024.0
    ccv[:, C_HALF] = 0.5
    ccv[:, 8] = 1.0
    ccv[:, 9] = -1.0
    ccv[:, 10] = 1.0
    ccv[:, 11] = -1.0
    identv = np.eye(128, dtype=np.float32)
    onesv = np.ones((128, 128), dtype=bf)
    tidx = np.arange(512)[None, None, :]
    pidx = np.arange(128)[:, None, None]
    cidx = np.arange(4)[None, :, None]
    maskv = (tidx >= 128 * cidx + pidx).astype(bf)
    col = np.arange(I)
    gv = (np.arange(96)[:, None] ==
          (4 * (col // 128) + (col % 128) // 32)[None, :]).astype(bf)
    return ccv, identv, onesv, maskv, gv


def kernel(L0_attn_x, L0_attn_wj_w, L0_attn_wjv_w, L0_attn_wo_w,
           L0_ffn_w_gate, L0_ffn_w_up, L0_ffn_w_down, _dbg=(), _phase=None):
    x = np.ascontiguousarray(np.asarray(L0_attn_x, np.float32))
    wj = np.ascontiguousarray(np.asarray(L0_attn_wj_w, np.float32))
    wjv = np.ascontiguousarray(np.asarray(L0_attn_wjv_w, np.float32))
    wo = np.ascontiguousarray(np.asarray(L0_attn_wo_w, np.float32))
    wg = np.ascontiguousarray(np.asarray(L0_ffn_w_gate, np.float32))
    wu = np.ascontiguousarray(np.asarray(L0_ffn_w_up, np.float32))
    wd = np.ascontiguousarray(np.asarray(L0_ffn_w_down, np.float32))

    ccv, identv, onesv, maskv, gv = _host_consts()
    common = {
        "wupT": np.ascontiguousarray(wu.T),
        "wgateT": np.ascontiguousarray(wg.T),
        "wdownT": np.ascontiguousarray(wd.T),
        "wupN": wu, "wgateN": wg, "wdownN": wd,
        "wjT": np.ascontiguousarray(wj.T),
        "wjvT": np.ascontiguousarray(wjv.T),
        "woT": np.ascontiguousarray(wo.T),
        "identD": identv, "onesD": onesv, "masksD": maskv,
        "g96D": gv, "constD": ccv,
    }
    in_maps = []
    for c in range(NCORE):
        m = dict(common)
        m["x0"] = np.ascontiguousarray(x[BPC * c: BPC * (c + 1)].reshape(N, E))
        in_maps.append(m)

    nc = build(dbg=_dbg, phase_limit=_phase)
    res = run_bass_kernel_spmd(nc, in_maps, core_ids=list(range(NCORE)))
    if _dbg or _phase is not None:
        return res
    return np.concatenate(
        [r["out"].reshape(BPC, T, E) for r in res.results], axis=0)
